# version 19
# baseline (speedup 1.0000x reference)
import math
import numpy as np
import concourse.bass as bass
import concourse.mybir as mybir
from concourse.bass_utils import run_bass_kernel_spmd
from contextlib import ExitStack

F32 = mybir.dt.float32
BF16 = mybir.dt.bfloat16
AF = mybir.ActivationFunctionType
ALU = mybir.AluOpType
AXX = mybir.AxisListType.X

D = 1024
INNER = 3072
DH = 4096
L = 4096
TOK = 2048
TPAD = 2176
NCORE = 8
EPS = 1e-5
PI = math.pi


class Seq:

    def __init__(self, nc):
        self.nc = nc
        self.engines = ["tensor", "vector", "scalar", "sync", "gpsimd"]
        self.sems = {}
        self.stages = []
        self.events = {}

    def alloc_sems(self, ctx):
        for e in self.engines + ["coll", "scl", "gpc", "vtk"]:
            self.sems[e] = ctx.enter_context(self.nc.semaphore(f"seq_{e}"))
        self.scl_tick = 0
        self.ticks = {"scl": 0, "vtk": 0}

    def tick(self, eng, ins, sem="vtk"):
        self.ticks[sem] += 1
        ins.then_inc(self.sems[sem], 1)
        eng.wait_ge(self.sems[sem], self.ticks[sem])

    def stage(self, engine, fn, dma=False, ndma=1, signal=None, after=(), nochain=False,
              coll=False):
        self.stages.append(dict(engine=engine, fn=fn, inc=16 * ndma if dma else 1,
                                dma=dma, signal=signal, after=tuple(after),
                                nochain=nochain, coll=coll))

    def build(self, block):
        per_engine = {e: [] for e in self.engines}
        counts = {e: 0 for e in self.engines}
        prev = None
        counts["coll"] = 0
        counts["gpc"] = 0
        prev_engine = None
        prev_same = {}
        for st in self.stages:
            e = st["engine"]
            waits = []
            if (not st["nochain"]) and (not st["coll"]) and prev is not None:
                waits.append(prev)
            if e in ("vector", "scalar") and e in prev_same:
                waits.append(prev_same[e])
            for name in st["after"]:
                waits.append(self.events[name])
            if st["coll"]:
                counts["coll"] += 1
                if st["signal"]:
                    self.events[st["signal"]] = ("coll", counts["coll"])
                per_engine[e].append((waits, st["fn"], 1, "coll", counts["coll"]))
                continue
            sname = "gpc" if (e == "gpsimd" and not st["dma"]) else e
            counts[sname] += st["inc"]
            if st["signal"]:
                self.events[st["signal"]] = (sname, counts[sname])
            per_engine[e].append((waits, st["fn"], st["inc"],
                                  st["dma"] or sname, counts[sname]))
            prev_same[e] = (sname, counts[sname])
            if not st["nochain"]:
                prev = (sname, counts[sname])
                prev_engine = e
        self.final = dict(counts)

        def make_body(engine):
            items = per_engine[engine]

            def body(eng):
                seen = {}
                for waits, fn, inc, isdma, post in items:
                    for (we, wv) in waits:
                        if seen.get(we, -1) < wv:
                            eng.wait_ge(self.sems[we], wv)
                            seen[we] = wv
                    if isdma == "coll":
                        last = fn(eng)
                        assert last is not None, f"coll stage on {engine} returned None"
                        last.then_inc(self.sems["coll"], inc)
                    elif isdma is True:
                        fn(eng, self.sems[engine])
                        eng.wait_ge(self.sems[engine], post)
                        seen[engine] = max(seen.get(engine, -1), post)
                    else:
                        sname = isdma
                        last = fn(eng)
                        assert last is not None, f"stage on {engine} returned None"
                        last.then_inc(self.sems[sname], inc)
            return body

        for e in self.engines:
            if per_engine[e]:
                getattr(block, e)(make_body(e))


def _conv_constants():
    n2 = np.arange(64, dtype=np.float64)[:, None]
    k2 = np.arange(128, dtype=np.float64)[None, :]
    th = 2 * np.pi * n2 * k2 / 128.0
    CS = np.concatenate([np.cos(th), -np.sin(th)], axis=1)
    CS = np.concatenate([CS, CS], axis=0)

    n1 = np.arange(64, dtype=np.float64)[:, None]
    th = 2 * np.pi * n1 * k2 / 8192.0
    Trep_re = np.tile(np.concatenate([np.cos(th)] * 2, axis=0), (1, 4))
    Trep_im = np.tile(np.concatenate([-np.sin(th)] * 2, axis=0), (1, 4))

    k1 = np.arange(64, dtype=np.float64)[None, :]
    th = 2 * np.pi * n1 * k1 / 64.0
    Wr = np.concatenate([np.cos(th)] * 2, axis=0)
    Wi = np.concatenate([-np.sin(th)] * 2, axis=0)

    k1c = np.arange(64, dtype=np.float64)[:, None]
    n1r = np.arange(64, dtype=np.float64)[None, :]
    th = 2 * np.pi * k1c * n1r / 64.0
    Vr, Vi = np.cos(th), np.sin(th)
    movP = np.concatenate([np.concatenate([Vr, Vi], axis=1)] * 2, axis=0)
    movM = np.concatenate([np.concatenate([-Vi, Vr], axis=1)] * 2, axis=0)

    k2c = np.arange(128, dtype=np.float64)[:, None]
    n2r = np.arange(64, dtype=np.float64)[None, :]
    U = np.zeros((128, 8192), dtype=np.float64)
    for i1 in range(64):
        th = 2 * np.pi * (i1 + 64 * n2r) * k2c / 8192.0
        U[:, i1 * 128: i1 * 128 + 64] = np.cos(th) / 8192.0
        U[:, i1 * 128 + 64: i1 * 128 + 128] = -np.sin(th) / 8192.0

    f32 = lambda a: np.ascontiguousarray(a, dtype=np.float32)
    return dict(c_CS=f32(CS), c_Tre=f32(Trep_re), c_Tim=f32(Trep_im),
                c_Wr=f32(Wr), c_Wi=f32(Wi), c_WiNeg=f32(-Wi),
                c_movP=f32(movP), c_movM=f32(movM), c_Ucons=f32(U))


def _z_const():
    t = np.linspace(0.0, 1.0, L)
    tr = np.linspace(0.0, L - 1.0, L)
    w = 2.0 * np.pi * tr / L
    z = np.stack([t, np.cos(1e-4 * w), -np.sin(1e-4 * w)], axis=0)
    return np.ascontiguousarray(z, dtype=np.float32)


def _decay_slice(core):
    max_d = np.log(0.01) / 0.3
    min_d = np.log(0.01) / 1.5
    deltas = np.abs(np.linspace(min_d, max_d, D))[128 * core: 128 * core + 128]
    t = np.linspace(0.0, 1.0, L)
    dec = np.exp(-t[None, :] * deltas[:, None])
    return np.ascontiguousarray(dec, dtype=np.float32)


def build_nc():
    nc = bass.Bass("TRN2", target_bir_lowering=False)
    ap = {}

    def din(name, shape):
        ap[name] = nc.dram_tensor(name, list(shape), F32, kind="ExternalInput").ap()

    din("u_s", (TPAD, D)); din("mask2", (128, 2))
    din("inwT", (24, 128, 1024)); din("inb_c", (128, 24))
    din("sw_c", (128, 72)); din("shb_c", (128, 24))
    din("outwT", (D, D)); din("outb_c", (128, 8))
    din("w1_t", (32, 128, 8 * 128)); din("w2_t", (32, 128, 8 * 128))
    din("h1_t", (32, 128, DH)); din("h2_t", (32, 128, DH))
    din("w3_t", (8, 128, DH))
    din("s1_c", (128, 32)); din("s2_c", (128, 32))
    din("sh1_c", (128, 32)); din("sh2_c", (128, 32)); din("s3_c", (128, 8))
    din("mlpw1", (3, 64)); din("mlpw2", (64, 64)); din("mlpw3", (64, 64))
    din("wo_s", (64, 128)); din("freq_c", (64, 1))
    din("b1_c", (64, 1)); din("b2_c", (64, 1)); din("b3_c", (64, 1))
    din("zT", (3, L)); din("decay_s", (128, L)); din("fbrep", (128, 512))
    din("fconst", (128, 8))
    for k, v in _conv_constants().items():
        din(k, v.shape)

    a1in = nc.dram_tensor("a1in", [1024, 2048], BF16).ap()
    a1out = nc.dram_tensor("a1out", [8192, 2048], BF16).ap()
    myvx = nc.dram_tensor("myvx", [8, 128, 2048], BF16).ap()
    a2in = nc.dram_tensor("a2in", [8, 128, 2048], BF16).ap()
    a2out = nc.dram_tensor("a2out", [8192, 2048], BF16).ap()
    ap["idx_g"] = nc.dram_tensor("idx_g", [128, 8], mybir.dt.int32,
                                 kind="ExternalInput").ap()
    ap["idx2_g"] = nc.dram_tensor("idx2_g", [128, 8], mybir.dt.int32,
                                  kind="ExternalInput").ap()
    x0sp = nc.dram_tensor("x0sp", [128, 16384], BF16).ap()
    x1sp = nc.dram_tensor("x1sp", [128, 16384], F32).ap()
    kscr = nc.dram_tensor("kscr", [128, L], BF16).ap()
    outp = nc.dram_tensor("outp", [TOK, D], F32, kind="ExternalOutput").ap()
    w1b = nc.dram_tensor("w1b", [32, 128, 1024], BF16).ap()
    w2b = nc.dram_tensor("w2b", [32, 128, 1024], BF16).ap()
    h1b = nc.dram_tensor("h1b", [32, 128, 4096], BF16).ap()
    h2b = nc.dram_tensor("h2b", [32, 128, 4096], BF16).ap()
    w3b = nc.dram_tensor("w3b", [8, 128, 4096], BF16).ap()

    with ExitStack() as ctx:
        sbt = lambda n, s, d: ctx.enter_context(nc.sbuf_tensor(n, s, d))

        seq = Seq(nc)
        seq.alloc_sems(ctx)

        cv_tasks = []
        for oj in range(0, 32, 8):
            cv_tasks.append((w1b[oj:oj + 8], ap["w1_t"][oj:oj + 8]))
            cv_tasks.append((w2b[oj:oj + 8], ap["w2_t"][oj:oj + 8]))
        for oj in range(32):
            cv_tasks.append((h1b[oj], ap["h1_t"][oj]))
        for oj in range(32):
            cv_tasks.append((h2b[oj], ap["h2_t"][oj]))
        for ej in range(0, 8, 2):
            cv_tasks.append((w3b[ej:ej + 2], ap["w3_t"][ej:ej + 2]))
        cv_state = {"i": 0}

        def emit_cv():
            i = cv_state["i"]
            if i >= len(cv_tasks):
                return
            cv_state["i"] = i + 1
            dst, src = cv_tasks[i]

            def s_cv(g, sem, dst=dst, src=src):
                g.dma_start(out=dst, in_=src).then_inc(sem, 16)
            seq.stage("gpsimd", s_cv, dma=True, ndma=1, nochain=True,
                      signal="cv_all" if i == len(cv_tasks) - 1 else None)

        CSc = sbt("CSc", [128, 256], BF16)
        Tre = sbt("Tre", [128, 512], BF16); Tim = sbt("Tim", [128, 512], BF16)
        Wrc = sbt("Wrc", [128, 64], BF16); Wic = sbt("Wic", [128, 64], BF16)
        WiN = sbt("WiN", [128, 64], BF16)
        mvP = sbt("mvP", [128, 128], BF16); mvM = sbt("mvM", [128, 128], BF16)
        Ucn = sbt("Ucn", [128, 8192], BF16)
        fbr = sbt("fbr", [128, 512], BF16)
        colbuf = sbt("colbuf", [128, 256], F32)
        colb2 = sbt("colb2", [128, 16], F32)
        mcol = sbt("mcol", [64, 8], F32)
        masks = sbt("masks", [128, 2], BF16)
        scr4 = sbt("scr4", [128, 1024], F32)
        stc = sbt("stc", [128, 16], F32)
        fcn = sbt("fcn", [128, 8], F32)
        idxs = sbt("idxs", [128, 8], mybir.dt.int32)
        idxs2 = sbt("idxs2", [128, 8], mybir.dt.int32)

        def s_const_loads(g, sem):
            pairs = [("c_CS", CSc), ("c_Tre", Tre), ("c_Tim", Tim),
                     ("c_Wr", Wrc), ("c_Wi", Wic), ("c_WiNeg", WiN),
                     ("c_movP", mvP), ("c_movM", mvM), ("c_Ucons", Ucn),
                     ("fbrep", fbr), ("mask2", masks)]
            for name, dst in pairs:
                g.dma_start(out=dst[:, :], in_=ap[name][:, :]).then_inc(sem, 16)
            for dst_sl, name in [(colbuf[:, 0:24], "inb_c"), (colbuf[:, 24:48], "shb_c"),
                                 (colbuf[:, 48:120], "sw_c"), (colbuf[:, 120:128], "outb_c"),
                                 (colbuf[:, 128:160], "s1_c"), (colbuf[:, 160:192], "s2_c"),
                                 (colbuf[:, 192:224], "sh1_c"), (colbuf[:, 224:256], "sh2_c"),
                                 (colb2[:, 0:8], "s3_c"), (mcol[:, 0:1], "freq_c"),
                                 (mcol[:, 1:2], "b1_c"), (mcol[:, 2:3], "b2_c"),
                                 (mcol[:, 3:4], "b3_c"), (fcn[:, :], "fconst"),
                                 (idxs[:, :], "idx_g"), (idxs2[:, :], "idx2_g")]:
                g.dma_start(out=dst_sl, in_=ap[name][:, :]).then_inc(sem, 16)
        seq.stage("gpsimd", s_const_loads, dma=True, ndma=27)

        def ln_stats(buf_ap):
            def s_red(v):
                return v.tensor_reduce(stc[:, 0:1], buf_ap, AXX, ALU.add)
            seq.stage("vector", s_red)

            def s_sq(s):
                return s.activation(scr4[:, :], buf_ap, AF.Square,
                                    accum_out=stc[:, 1:2])
            seq.stage("scalar", s_sq)

            def s_muA(v):
                v.tensor_scalar(stc[:, 2:3], stc[:, 0:1], fcn[:, 3:4], None, ALU.mult)
                return v.tensor_scalar(stc[:, 4:5], stc[:, 1:2], fcn[:, 3:4],
                                       None, ALU.mult)
            seq.stage("vector", s_muA)

            def s_muB(s):
                return s.activation(stc[:, 3:4], stc[:, 2:3], AF.Square)
            seq.stage("scalar", s_muB)

            def s_muC(v):
                return v.tensor_tensor(stc[:, 4:5], stc[:, 4:5], stc[:, 3:4],
                                       ALU.subtract)
            seq.stage("vector", s_muC)

            def s_inv(s):
                seq.scl_tick += 1
                s.activation(stc[:, 5:6], stc[:, 4:5], AF.Ln,
                             bias=fcn[:, 0:1]).then_inc(seq.sems["scl"], 1)
                s.wait_ge(seq.sems["scl"], seq.scl_tick)
                return s.activation(stc[:, 6:7], stc[:, 5:6], AF.Exp,
                                    scale=fcn[:, 1:2])
            seq.stage("scalar", s_inv)

        with ExitStack() as p1:
            sb1 = lambda n, s, d: p1.enter_context(nc.sbuf_tensor(n, s, d))
            ps1 = lambda n, s: p1.enter_context(nc.psum_tensor(n, s, F32))
            ln1T = sb1("ln1T", [128, 8 * TPAD], BF16)
            inwb = [sb1(f"inwb{i}", [128, 1024], BF16) for i in range(2)]
            utile = sb1("utile", [128, 1024], F32)
            ln1t = sb1("ln1t", [128, 1024], BF16)
            upb = [sb1(f"upb{i}", [128, TPAD], BF16) for i in range(2)]
            tap1 = sb1("tap1", [128, 2048], BF16)
            tap2 = sb1("tap2", [128, 2048], BF16)
            x1sb = sb1("x1sb", [128, 8 * 2048], BF16)
            x0c = [sb1(f"x0c{i}", [128, 2048], BF16) for i in range(2)]
            vxc = [sb1(f"vxc{i}", [128, 2048], BF16) for i in range(2)]
            zTb = sb1("zTb", [3, L], BF16)
            mw1 = sb1("mw1", [3, 64], BF16)
            mw2 = sb1("mw2", [64, 64], BF16); mw3 = sb1("mw3", [64, 64], BF16)
            wos = sb1("wos", [64, 128], BF16)
            dcy = sb1("dcy", [128, L], BF16)
            harg = sb1("harg", [64, 512], F32)
            hri = sb1("hri", [64, 512], mybir.dt.int32)
            hrf = sb1("hrf", [64, 512], F32)
            hsb = sb1("hsb", [64, 512], BF16)
            ksb = sb1("ksb", [128, 512], BF16)
            ups_m = ps1("ups_m", [128, 2048])
            ups_t = ps1("ups_t", [128, 128])
            hps = ps1("hps", [64, 512])
            kps = ps1("kps", [128, 512])

            for tt in range(17):
                def s_uload(sy, sem, tt=tt):
                    sy.dma_start(out=utile[:, :],
                                 in_=ap["u_s"][tt * 128:(tt + 1) * 128, :]
                                 ).then_inc(sem, 16)
                seq.stage("sync", s_uload, dma=True)

                ln_stats(utile[:, :])

                def s_apply(v):
                    return v.tensor_scalar(ln1t[:, :], utile[:, :], stc[:, 2:3],
                                           stc[:, 6:7], ALU.subtract, ALU.mult)
                seq.stage("vector", s_apply)

                def s_tdma(sy, sem, tt=tt):
                    for dk in range(8):
                        sy.dma_start(out=ln1T[:, dk * TPAD + tt * 128:
                                              dk * TPAD + (tt + 1) * 128],
                                     in_=ln1t[:, dk * 128:(dk + 1) * 128],
                                     transpose=True).then_inc(sem, 16)
                seq.stage("sync", s_tdma, dma=True, ndma=8,
                          signal="ln1T_ok" if tt == 16 else None)

            cj_order = list(range(8, 24)) + list(range(0, 8))
            for ci, cj in enumerate(cj_order):
                kind = "x1" if cj < 16 and cj >= 8 else ("v" if cj >= 16 else "x0")

                def s_iwl(g, sem, cj=cj, ci=ci):
                    g.dma_start(out=inwb[ci % 2][:, :],
                                in_=ap["inwT"][cj, :, :]).then_inc(sem, 16)
                seq.stage("gpsimd", s_iwl, dma=True, ndma=1, nochain=True,
                          signal=f"iwl{ci}",
                          after=(f"iwm{ci - 2}",) if ci >= 2 else ())

                def s_mm(t, cj=cj, ci=ci):
                    for dk in range(8):
                        stt = inwb[ci % 2][:, dk * 128:(dk + 1) * 128]
                        for ni in range(4):
                            t.matmul(ups_m[:, ni * 512:(ni + 1) * 512], stt,
                                     ln1T[:, dk * TPAD + ni * 512:
                                          dk * TPAD + (ni + 1) * 512],
                                     start=(dk == 0), stop=(dk == 7))
                        last = t.matmul(ups_t[:, :], stt,
                                        ln1T[:, dk * TPAD + 2048: dk * TPAD + 2176],
                                        start=(dk == 0), stop=(dk == 7))
                    return last
                aft = [f"iwl{ci}"]
                aft.append("ln1T_ok" if ci == 0 else f"sevac{ci - 1}")
                seq.stage("tensor", s_mm, nochain=True, after=tuple(aft),
                          signal=f"iwm{ci}")

                def s_evac(s, cj=cj, ci=ci):
                    ub = upb[ci % 2]
                    s.activation(ub[:, 0:2048], ups_m[:, :], AF.Identity,
                                 bias=colbuf[:, cj:cj + 1])
                    return s.activation(ub[:, 2048:2176], ups_t[:, :], AF.Identity,
                                        bias=colbuf[:, cj:cj + 1])
                aft = [f"iwm{ci}"] + ([f"staps{ci - 2}"] if ci >= 2 else [])
                seq.stage("scalar", s_evac, nochain=True, after=tuple(aft),
                          signal=f"sevac{ci}")

                def s_taps(v, cj=cj, ci=ci, kind=kind):
                    ub = upb[ci % 2]
                    i0 = v.tensor_tensor(ub[:, 0:2], ub[:, 0:2], masks[:, :],
                                         ALU.mult)
                    seq.tick(v, i0)
                    i1 = v.tensor_scalar(tap1[:, :], ub[:, 0:2048],
                                         colbuf[:, 48 + cj:49 + cj],
                                         colbuf[:, 24 + cj:25 + cj], ALU.mult, ALU.add)
                    seq.tick(v, i1)
                    i2 = v.scalar_tensor_tensor(tap2[:, :], ub[:, 1:2049],
                                                colbuf[:, 72 + cj:73 + cj], tap1[:, :],
                                                ALU.mult, ALU.add)
                    seq.tick(v, i2)
                    dst = (x1sb[:, (cj - 8) * 2048:(cj - 7) * 2048] if kind == "x1"
                           else x0c[ci % 2][:, :] if kind == "x0" else tap1[:, :])
                    last = v.scalar_tensor_tensor(dst, ub[:, 2:2050],
                                                  colbuf[:, 96 + cj:97 + cj],
                                                  tap2[:, :], ALU.mult, ALU.add)
                    if kind == "v":
                        seq.tick(v, last)
                        last = v.tensor_tensor(
                            vxc[ci % 2][:, :], tap1[:, :],
                            x1sb[:, (cj - 16) * 2048:(cj - 15) * 2048], ALU.mult)
                    return last
                aft = [f"sevac{ci}"]
                if ci >= 10:
                    aft.append("a1in_ok" if ci - 2 == 15 else f"cst{ci - 2}")
                seq.stage("vector", s_taps, nochain=True, after=tuple(aft),
                          signal=f"staps{ci}")

                if kind == "v":
                    def s_vx_store(sy, sem, cj=cj, ci=ci):
                        sy.dma_start(out=a1in[(cj - 16) * 128:(cj - 15) * 128, :],
                                     in_=vxc[ci % 2][:, :]).then_inc(sem, 16)
                    seq.stage("sync", s_vx_store, dma=True, nochain=True,
                              after=(f"staps{ci}",),
                              signal="a1in_ok" if cj == 23 else f"cst{ci}")
                if kind == "x0":
                    def s_x0_store(sy, sem, cj=cj, ci=ci):
                        sy.dma_start(out=x0sp[:, cj * 2048:(cj + 1) * 2048],
                                     in_=x0c[ci % 2][:, :]).then_inc(sem, 16)
                    seq.stage("sync", s_x0_store, dma=True, nochain=True,
                              after=(f"staps{ci}",), signal=f"cst{ci}")

                if cj == 23:
                    def s_a2a1(g):
                        return g.collective_compute(
                            "AllGather", ALU.bypass,
                            replica_groups=[list(range(NCORE))],
                            ins=[a1in[:, :].opt()], outs=[a1out[:, :].opt()])
                    seq.stage("gpsimd", s_a2a1, signal="a2a1_done", coll=True,
                              after=("a1in_ok",))

            def s_kload(g, sem):
                for name, dst in [("zT", zTb), ("mlpw1", mw1), ("mlpw2", mw2),
                                  ("mlpw3", mw3), ("wo_s", wos), ("decay_s", dcy)]:
                    g.dma_start(out=dst[:, :], in_=ap[name][:, :]).then_inc(sem, 16)
            seq.stage("gpsimd", s_kload, dma=True, ndma=6)

            def s_fb(v):
                v.tensor_tensor(mcol[:, 4:5], mcol[:, 0:1], mcol[:, 1:2], ALU.mult)
                v.tensor_tensor(mcol[:, 5:6], mcol[:, 0:1], mcol[:, 2:3], ALU.mult)
                return v.tensor_tensor(mcol[:, 6:7], mcol[:, 0:1], mcol[:, 3:4], ALU.mult)
            seq.stage("vector", s_fb)

            for tc in range(8):
                def s_l1(t, tc=tc):
                    return t.matmul(hps[:, :], mw1[:, :],
                                    zTb[:, tc * 512:(tc + 1) * 512],
                                    start=True, stop=True)
                seq.stage("tensor", s_l1)
                for li in range(3):
                    if li > 0:
                        wmat = mw2 if li == 1 else mw3

                        def s_lx(t, wmat=wmat):
                            return t.matmul(hps[:, :], wmat[:, :], hsb[:, :],
                                            start=True, stop=True)
                        seq.stage("tensor", s_lx)
                    fbcol = 4 + li

                    def s_arg(v, fbcol=fbcol):
                        i1 = v.tensor_scalar(harg[:, :], hps[:, :], mcol[:, 0:1],
                                             mcol[:, fbcol:fbcol + 1], ALU.mult, ALU.add)
                        seq.tick(v, i1)
                        i2 = v.tensor_scalar(hrf[:, :], harg[:, :], fcn[0:64, 6:7],
                                             None, ALU.mult)
                        seq.tick(v, i2)
                        i3 = v.tensor_copy(hri[:, :], hrf[:, :])
                        seq.tick(v, i3)
                        i4 = v.tensor_copy(hrf[:, :], hri[:, :])
                        seq.tick(v, i4)
                        return v.scalar_tensor_tensor(harg[:, :], hrf[:, :],
                                                      fcn[0:64, 7:8], harg[:, :],
                                                      ALU.mult, ALU.add)
                    seq.stage("vector", s_arg)

                    def s_sin(s):
                        return s.activation(hsb[:, :], harg[:, :], AF.Sin)
                    seq.stage("scalar", s_sin)

                def s_wo(t):
                    return t.matmul(kps[:, :], wos[:, :], hsb[:, :],
                                    start=True, stop=True)
                seq.stage("tensor", s_wo)

                def s_kmul(v, tc=tc):
                    return v.tensor_tensor(ksb[:, :], kps[:, :],
                                           dcy[:, tc * 512:(tc + 1) * 512], ALU.mult)
                seq.stage("vector", s_kmul)

                def s_kst(sy, sem, tc=tc):
                    sy.dma_start(out=kscr[:, tc * 512:(tc + 1) * 512],
                                 in_=ksb[:, :]).then_inc(sem, 16)
                seq.stage("sync", s_kst, dma=True)

        with ExitStack() as p3:
            sb3 = lambda n, s, d: p3.enter_context(nc.sbuf_tensor(n, s, d))
            ps3 = lambda n, s: p3.enter_context(nc.psum_tensor(n, s, F32))
            Kre = sb3("Kre", [128, 16384], BF16)
            Kim = sb3("Kim", [128, 16384], BF16)
            Draw = sb3("Draw", [128, 32768], BF16)
            Are = sb3("Are", [128, 512], BF16); Aim = sb3("Aim", [128, 512], BF16)
            Apre = sb3("Apre", [128, 512], BF16); Apim = sb3("Apim", [128, 512], BF16)
            tv1 = sb3("tv1", [128, 512], F32); tv2 = sb3("tv2", [128, 512], F32)
            tv3 = sb3("tv3", [128, 512], F32); tv4 = sb3("tv4", [128, 512], F32)
            Xre_sb = sb3("Xre_sb", [128, 512], BF16)
            Xim_sb = sb3("Xim_sb", [128, 512], BF16)
            Yre = sb3("Yre", [128, 512], BF16); Yim = sb3("Yim", [128, 512], BF16)
            gxa = sb3("gxa", [128, 2048], BF16)
            A_ps = ps3("A_ps", [128, 1024])
            Xre_ps = ps3("Xre_ps", [128, 512]); Xim_ps = ps3("Xim_ps", [128, 512])
            D_ps = ps3("D_ps", [128, 1024]); y_ps = ps3("y_ps", [128, 512])

            def fwd_group(xbuf, g, fg, first_sig, kside):
                def s_step1(t, g=g):
                    for cl in range(4):
                        cs = (4 * g + cl) * 64
                        t.matmul(A_ps[0:64, cl * 256:(cl + 1) * 256],
                                 xbuf[0:64, cs:cs + 64], CSc[0:64, :],
                                 start=True, stop=True, tile_position=(0, 0))
                        last = t.matmul(A_ps[64:128, cl * 256:(cl + 1) * 256],
                                        xbuf[64:128, cs:cs + 64], CSc[64:128, :],
                                        start=True, stop=True, tile_position=(64, 64))
                    return last
                aft = [] if fg == 0 else [f"tw_{fg - 1}"]
                if first_sig:
                    aft.append(first_sig)
                seq.stage("tensor", s_step1, nochain=True, after=tuple(aft),
                          signal=f"st1_{fg}")

                def s_tw(v):
                    apv = A_ps[:, :].rearrange("p (c r k) -> p c r k", c=4, r=2, k=128)
                    Apr = apv[:, :, 0, :]
                    Api = apv[:, :, 1, :]
                    t1v = tv1[:, :].rearrange("p (c k) -> p c k", c=4)
                    t2v = tv2[:, :].rearrange("p (c k) -> p c k", c=4)
                    t3v = tv3[:, :].rearrange("p (c k) -> p c k", c=4)
                    t4v = tv4[:, :].rearrange("p (c k) -> p c k", c=4)
                    Trev = Tre[:, :].rearrange("p (c k) -> p c k", c=4)
                    Timv = Tim[:, :].rearrange("p (c k) -> p c k", c=4)
                    v.tensor_tensor(t1v, Apr, Trev, ALU.mult)
                    v.tensor_tensor(t2v, Api, Timv, ALU.mult)
                    v.tensor_tensor(t3v, Apr, Timv, ALU.mult)
                    i4 = v.tensor_tensor(t4v, Api, Trev, ALU.mult)
                    seq.tick(v, i4)
                    v.tensor_tensor(Apre[:, :], tv1[:, :], tv2[:, :], ALU.subtract)
                    return v.tensor_tensor(Apim[:, :], tv3[:, :], tv4[:, :], ALU.add)
                aft = [f"st1_{fg}"] + ([f"st3_{fg - 1}"] if fg > 0 else [])
                seq.stage("vector", s_tw, nochain=True, after=tuple(aft),
                          signal=f"tw_{fg}")

                def s_step3(t):
                    for (lo, hi, tp) in ((0, 64, (0, 0)), (64, 128, (64, 64))):
                        t.matmul(Xre_ps[lo:hi, :], Wrc[lo:hi, :], Apre[lo:hi, :],
                                 start=True, stop=False, tile_position=tp)
                        t.matmul(Xre_ps[lo:hi, :], WiN[lo:hi, :], Apim[lo:hi, :],
                                 start=False, stop=True, tile_position=tp)
                        t.matmul(Xim_ps[lo:hi, :], Wic[lo:hi, :], Apre[lo:hi, :],
                                 start=True, stop=False, tile_position=tp)
                        last = t.matmul(Xim_ps[lo:hi, :], Wrc[lo:hi, :], Apim[lo:hi, :],
                                        start=False, stop=True, tile_position=tp)
                    return last
                aft = [f"tw_{fg}"]
                if fg > 0:
                    aft.append(f"kx_{fg - 1}" if fg <= 16 else f"pw_{fg - 1}")
                seq.stage("tensor", s_step3, nochain=True, after=tuple(aft),
                          signal=f"st3_{fg}")

            for j in range(8):
                def s_gath(g, sem, j=j):
                    g.indirect_dma_start(
                        out=gxa[:, :], out_offset=None,
                        in_=a1out[:, :],
                        in_offset=bass.IndirectOffsetOnAxis(idxs[:, j:j + 1], 0),
                    ).then_inc(sem, 16)
                seq.stage("gpsimd", s_gath, dma=True,
                          after=("a2a1_done",) if j == 0 else ())

                def s_gsto(sy, sem, j=j):
                    sy.dma_start(out=myvx[j, :, :], in_=gxa[:, :]).then_inc(sem, 16)
                seq.stage("sync", s_gsto, dma=True)

            with ExitStack() as p3k:
                xk = p3k.enter_context(nc.sbuf_tensor("xk", [128, 4096], BF16))

                def s_xkload(sy, sem):
                    src0 = kscr[0:64, :].rearrange("c (p n1) -> p c n1", p=64)
                    src1 = kscr[64:128, :].rearrange("c (p n1) -> p c n1", p=64)
                    for ci in range(4):
                        sy.dma_start(out=xk[0:64, ci * 1024:(ci + 1) * 1024],
                                     in_=src0[:, ci * 16:(ci + 1) * 16, :]
                                     ).then_inc(sem, 16)
                        sy.dma_start(out=xk[64:128, ci * 1024:(ci + 1) * 1024],
                                     in_=src1[:, ci * 16:(ci + 1) * 16, :]
                                     ).then_inc(sem, 16)
                seq.stage("sync", s_xkload, dma=True, ndma=8, signal="xk_ok")

                for g_ in range(16):
                    fwd_group(xk, g_, g_, "xk_ok" if g_ == 0 else None, True)
                    emit_cv()

                    def s_kx(s, g_=g_):
                        s.copy(Draw[:, g_ * 1024: g_ * 1024 + 512], Xre_ps[:, :])
                        return s.copy(Draw[:, g_ * 1024 + 512:(g_ + 1) * 1024],
                                      Xim_ps[:, :])
                    seq.stage("scalar", s_kx, nochain=True,
                              after=(f"st3_{g_}",), signal=f"kx_{g_}")

            xin = sb3("xin", [128, 8192], BF16)
            v2sb = sb3("v2sb", [128, 8192], BF16)

            kr = Draw[:, 0:16384].rearrange("p (g two ck) -> p g two ck",
                                            g=16, two=2, ck=512)

            for sbi in range(2):
                def s_xinload(sy, sem, sbi=sbi):
                    for h in range(2):
                        for q in range(2):
                            shard = 4 * sbi + 2 * h + q
                            src = myvx[shard, :, :].rearrange(
                                "c (p n1) -> p c n1", p=32)
                            for ci in range(4):
                                sy.dma_start(
                                    out=xin[h * 64 + q * 32: h * 64 + q * 32 + 32,
                                            ci * 2048:(ci + 1) * 2048],
                                    in_=src[:, ci * 32:(ci + 1) * 32, :]
                                    ).then_inc(sem, 16)
                seq.stage("sync", s_xinload, dma=True, ndma=16,
                          after=("st1_15",) if sbi == 0 else (),
                          signal=f"xin_ok{sbi}")

                if sbi == 0:
                    kdup = 0
                    for (plo, phi, coff) in ((0, 64, 0), (64, 128, 8192)):
                        for (dlo, dhi) in ((0, 64), (64, 128)):
                            def s_dup(sy, sem, plo=plo, phi=phi, coff=coff,
                                      dlo=dlo, dhi=dhi):
                                sy.dma_start(out=Kre[dlo:dhi, coff:coff + 8192],
                                             in_=kr[plo:phi, :, 0, :]).then_inc(sem, 16)
                                sy.dma_start(out=Kim[dlo:dhi, coff:coff + 8192],
                                             in_=kr[plo:phi, :, 1, :]).then_inc(sem, 16)
                            seq.stage("sync", s_dup, dma=True, ndma=2,
                                      after=("kx_15",) if kdup == 0 else (),
                                      signal="Kready" if kdup == 3 else None)
                            kdup += 1

                for g_ in range(32):
                    fg = 16 + sbi * 32 + g_
                    fwd_group(xin, g_, fg, f"xin_ok{sbi}" if g_ == 0 else None, False)
                    emit_cv()

                    def s_pw(v, g_=g_):
                        ko = 4 * g_ * 128
                        v.tensor_tensor(tv1[:, :], Xre_ps[:, :], Kre[:, ko:ko + 512],
                                        ALU.mult)
                        v.tensor_tensor(tv2[:, :], Xim_ps[:, :], Kim[:, ko:ko + 512],
                                        ALU.mult)
                        v.tensor_tensor(tv3[:, :], Xre_ps[:, :], Kim[:, ko:ko + 512],
                                        ALU.mult)
                        i4 = v.tensor_tensor(tv4[:, :], Xim_ps[:, :], Kre[:, ko:ko + 512],
                                             ALU.mult)
                        seq.tick(v, i4)
                        v.tensor_tensor(Yre[:, :], tv1[:, :], tv2[:, :], ALU.subtract)
                        return v.tensor_tensor(Yim[:, :], tv3[:, :], tv4[:, :], ALU.add)
                    aft = [f"st3_{fg}"]
                    if fg == 16:
                        aft.append("Kready")
                    if fg > 16:
                        aft.append(f"stA_{fg - 1}")
                    seq.stage("vector", s_pw, nochain=True, after=tuple(aft),
                              signal=f"pw_{fg}")

                    def s_stepA(t):
                        for cl in range(4):
                            for h, (lo, hi) in enumerate(((0, 64), (64, 128))):
                                sl = cl if h == 0 else 4 + cl
                                t.matmul(D_ps[:, sl * 128:(sl + 1) * 128],
                                         Yre[lo:hi, cl * 128:(cl + 1) * 128],
                                         mvP[lo:hi, :], start=True, stop=False,
                                         tile_position=(lo, 0))
                                last = t.matmul(D_ps[:, sl * 128:(sl + 1) * 128],
                                                Yim[lo:hi, cl * 128:(cl + 1) * 128],
                                                mvM[lo:hi, :], start=False, stop=True,
                                                tile_position=(lo, 0))
                        return last
                    aft = [f"pw_{fg}"] + ([f"evD_{fg - 1}"] if fg > 16 else [])
                    seq.stage("tensor", s_stepA, nochain=True, after=tuple(aft),
                              signal=f"stA_{fg}")

                    def s_evacD(s, g_=g_):
                        return s.copy(Draw[:, g_ * 1024:(g_ + 1) * 1024], D_ps[:, :])
                    aft = [f"stA_{fg}"]
                    if fg == 16:
                        aft.append("Kready")
                    if sbi == 1 and g_ == 0:
                        aft.append("stC0_15")
                    seq.stage("scalar", s_evacD, nochain=True, after=tuple(aft),
                              signal=f"evD_{fg}")

                Draw4 = Draw[:, :].rearrange("p (g sl two n1) -> p g sl two n1",
                                             g=32, sl=8, two=2, n1=64)
                xin3 = xin[:, :].rearrange("p (c n1) -> p c n1", c=128)
                v23 = v2sb[:, :].rearrange("p (c n1) -> p c n1", c=128)
                for n1b in range(16):
                    def s_stepC(t, n1b=n1b):
                        for n1l in range(4):
                            i1 = n1b * 4 + n1l
                            ure = Ucn[:, i1 * 128: i1 * 128 + 64]
                            uim = Ucn[:, i1 * 128 + 64: i1 * 128 + 128]
                            ysl = lambda lo, hi: y_ps[lo:hi, n1l * 128:(n1l + 1) * 128]
                            t.matmul(ysl(0, 64), ure, Draw4[:, :, 0:4, 0, i1],
                                     start=True, stop=False, tile_position=(0, 0))
                            t.matmul(ysl(0, 64), uim, Draw4[:, :, 0:4, 1, i1],
                                     start=False, stop=True, tile_position=(0, 0))
                            t.matmul(ysl(64, 128), ure, Draw4[:, :, 4:8, 0, i1],
                                     start=True, stop=False, tile_position=(0, 64))
                            last = t.matmul(ysl(64, 128), uim, Draw4[:, :, 4:8, 1, i1],
                                            start=False, stop=True,
                                            tile_position=(0, 64))
                        return last
                    aft = ([f"evD_{16 + sbi * 32 + 31}"] if n1b == 0
                           else [f"gate{sbi}_{n1b - 1}"])
                    if sbi == 1 and n1b == 0:
                        aft.append("gate0_15")
                    seq.stage("tensor", s_stepC, nochain=True, after=tuple(aft),
                              signal=f"stC{sbi}_{n1b}")

                    def s_gate(v, n1b=n1b):
                        t1v = tv1[:, :].rearrange("p (n1l c) -> p n1l c", n1l=4)
                        xslT = xin3[:, :, n1b * 4:n1b * 4 + 4].rearrange("p c n -> p n c")
                        fbv = fbr[:, :].rearrange("p (n1l c) -> p n1l c", n1l=4)
                        i1 = v.tensor_tensor(t1v, xslT, fbv, ALU.mult)
                        seq.tick(v, i1)
                        dstT = v23[:, :, n1b * 4:n1b * 4 + 4].rearrange("p c n -> p n c")
                        ypv = y_ps[:, :].rearrange("p (n1l c) -> p n1l c", n1l=4)
                        return v.tensor_tensor(dstT, ypv, t1v, ALU.add)
                    aft = [f"stC{sbi}_{n1b}"]
                    if sbi == 1 and n1b == 0:
                        aft.append("a2in_ok0")
                    seq.stage("vector", s_gate, nochain=True,
                              after=tuple(aft), signal=f"gate{sbi}_{n1b}")

                def s_v2store(sy, sem, sbi=sbi):
                    for h in range(2):
                        for q in range(2):
                            shard = 4 * sbi + 2 * h + q
                            dst = a2in[shard, :, :].rearrange("c (p n1) -> p c n1", p=32)
                            for ci in range(4):
                                sy.dma_start(
                                    out=dst[:, ci * 32:(ci + 1) * 32, :],
                                    in_=v2sb[h * 64 + q * 32: h * 64 + q * 32 + 32,
                                             ci * 2048:(ci + 1) * 2048]
                                    ).then_inc(sem, 16)
                seq.stage("sync", s_v2store, dma=True, ndma=16,
                          after=(f"gate{sbi}_15",), signal=f"a2in_ok{sbi}")

                def s_a2a2(g, sbi=sbi):
                    return g.collective_compute(
                        "AllGather", ALU.bypass, replica_groups=[list(range(NCORE))],
                        ins=[a2in[4 * sbi:4 * sbi + 4, :, :].opt()],
                        outs=[a2out[4096 * sbi:4096 * (sbi + 1), :].opt()])
                seq.stage("gpsimd", s_a2a2, signal=f"a2a2_done{sbi}", coll=True,
                          after=(f"a2in_ok{sbi}",))

        with ExitStack() as p45:
            ln3T = p45.enter_context(nc.sbuf_tensor("ln3T", [128, 16384], BF16))

            with ExitStack() as p4:
                sb4 = lambda n, s, d: p4.enter_context(nc.sbuf_tensor(n, s, d))
                ps4 = lambda n, s: p4.enter_context(nc.psum_tensor(n, s, F32))
                yvsb = sb4("yvsb", [128, 16384], BF16)
                x0c4 = sb4("x0c4", [128, 2048], BF16)
                v2c = sb4("v2c", [128, 2048], BF16)
                owsb = sb4("owsb", [128, 8192], BF16)
                attnT = sb4("attnT", [128, 16384], BF16)
                attc = [sb4(f"attc{i}", [128, 2048], BF16) for i in range(2)]
                utile4 = [sb4(f"utile4{i}", [128, 1024], F32) for i in range(2)]
                ln3t = [sb4(f"ln3t{i}", [128, 1024], BF16) for i in range(2)]
                x1half = sb4("x1half", [128, 8192], F32)
                stc16 = sb4("stc16", [128, 128], F32)
                o_ps = [ps4(f"o_ps{i}", [128, 2048]) for i in range(2)]

                def s_owload(g, sem):
                    for dk in range(8):
                        g.dma_start(out=owsb[:, dk * 1024:(dk + 1) * 1024],
                                    in_=ap["outwT"][dk * 128:(dk + 1) * 128, :]
                                    ).then_inc(sem, 16)
                seq.stage("gpsimd", s_owload, dma=True, ndma=8, nochain=True,
                          signal="ow_ok")

                for ck in range(8):
                    def s_v2g(g, sem, ck=ck):
                        g.indirect_dma_start(
                            out=v2c[:, :], out_offset=None,
                            in_=a2out[:, :],
                            in_offset=bass.IndirectOffsetOnAxis(idxs2[:, ck:ck + 1], 0),
                        ).then_inc(sem, 16)
                    seq.stage("gpsimd", s_v2g, dma=True,
                              after=("a2a2_done0", "a2a2_done1") if ck == 0 else ())

                    def s_v2l(sy, sem, ck=ck):
                        sy.dma_start(out=x0c4[:, :],
                                     in_=x0sp[:, ck * 2048:(ck + 1) * 2048]
                                     ).then_inc(sem, 16)
                    seq.stage("sync", s_v2l, dma=True)

                    def s_yv(v, ck=ck):
                        return v.tensor_tensor(yvsb[:, ck * 2048:(ck + 1) * 2048],
                                               v2c[:, :], x0c4[:, :], ALU.mult)
                    seq.stage("vector", s_yv, signal="yv7" if ck == 7 else None)

                for ej in range(8):
                    p = ej % 2

                    def s_omm(t, ej=ej, p=p):
                        for ck in range(8):
                            stt = owsb[:, ck * 1024 + ej * 128:
                                       ck * 1024 + (ej + 1) * 128]
                            for ni in range(4):
                                last = t.matmul(o_ps[p][:, ni * 512:(ni + 1) * 512],
                                                stt,
                                                yvsb[:, ck * 2048 + ni * 512:
                                                     ck * 2048 + (ni + 1) * 512],
                                                start=(ck == 0), stop=(ck == 7))
                        return last
                    aft = ["ow_ok", "yv7"] if ej == 0 else [f"oev{ej - 2}"] if ej >= 2 else ["ow_ok", "yv7"]
                    seq.stage("tensor", s_omm, nochain=True, after=tuple(aft),
                              signal=f"omm{ej}")

                    def s_oevac(s, ej=ej, p=p):
                        return s.activation(attc[p][:, :], o_ps[p][:, :], AF.Identity,
                                            bias=colbuf[:, 120 + ej:121 + ej])
                    aft = [f"omm{ej}"] + ([f"otd{ej - 2}"] if ej >= 2 else [])
                    seq.stage("scalar", s_oevac, nochain=True, after=tuple(aft),
                              signal=f"oev{ej}")

                    def s_otdma(sy, sem, ej=ej, p=p):
                        for tt in range(16):
                            sy.dma_start(out=attnT[:, tt * 1024 + ej * 128:
                                                   tt * 1024 + (ej + 1) * 128],
                                         in_=attc[p][:, tt * 128:(tt + 1) * 128],
                                         transpose=True).then_inc(sem, 16)
                    seq.stage("sync", s_otdma, dma=True, ndma=16, nochain=True,
                              after=(f"oev{ej}",), signal=f"otd{ej}")

                at3 = attnT[:, :].rearrange("p (tt d) -> p tt d", tt=16)
                sq3 = yvsb[:, :].rearrange("p (tt d) -> p tt d", tt=16)

                def s_sqA(s):
                    return s.activation(yvsb[:, :], attnT[:, :], AF.Square)
                seq.stage("scalar", s_sqA, after=("otd7",))

                def s_redA(v):
                    v.tensor_reduce(stc16[:, 0:16], at3, AXX, ALU.add)
                    return v.tensor_reduce(stc16[:, 16:32], sq3, AXX, ALU.add)
                seq.stage("vector", s_redA)

                def s_muA4(v):
                    v.tensor_scalar(stc16[:, 32:48], stc16[:, 0:16], fcn[:, 3:4],
                                    None, ALU.mult)
                    return v.tensor_scalar(stc16[:, 48:64], stc16[:, 16:32],
                                           fcn[:, 3:4], None, ALU.mult)
                seq.stage("vector", s_muA4)

                def s_msq4(s):
                    return s.activation(stc16[:, 64:80], stc16[:, 32:48], AF.Square)
                seq.stage("scalar", s_msq4)

                def s_var4(v):
                    return v.tensor_tensor(stc16[:, 80:96], stc16[:, 48:64],
                                           stc16[:, 64:80], ALU.subtract)
                seq.stage("vector", s_var4)

                def s_inv4(s):
                    seq.scl_tick += 1
                    s.activation(stc16[:, 112:128], stc16[:, 80:96], AF.Ln,
                                 bias=fcn[:, 0:1]).then_inc(seq.sems["scl"], 1)
                    s.wait_ge(seq.sems["scl"], seq.scl_tick)
                    return s.activation(stc16[:, 96:112], stc16[:, 112:128], AF.Exp,
                                        scale=fcn[:, 1:2])
                seq.stage("scalar", s_inv4)

                for half in range(2):
                    for tl in range(8):
                        tt = half * 8 + tl

                        def s_url(sy, sem, tt=tt, tl=tl):
                            sy.dma_start(out=utile4[tl % 2][:, :],
                                         in_=ap["u_s"][2 + tt * 128: 2 + (tt + 1) * 128, :]
                                         ).then_inc(sem, 16)
                        seq.stage("sync", s_url, dma=True)

                        def s_res(v, tt=tt, tl=tl):
                            att_t = attnT[:, tt * 1024:(tt + 1) * 1024]
                            i1 = v.tensor_scalar(scr4[:, :], att_t,
                                                 stc16[:, 32 + tt:33 + tt],
                                                 stc16[:, 96 + tt:97 + tt],
                                                 ALU.subtract, ALU.mult)
                            seq.tick(v, i1)
                            return v.tensor_tensor(x1half[:, tl * 1024:(tl + 1) * 1024],
                                                   scr4[:, :], utile4[tl % 2][:, :],
                                                   ALU.add)
                        seq.stage("vector", s_res)

                        def s_x1spl(sy, sem, tt=tt, tl=tl):
                            sy.dma_start(out=x1sp[:, tt * 1024:(tt + 1) * 1024],
                                         in_=x1half[:, tl * 1024:(tl + 1) * 1024]
                                         ).then_inc(sem, 16)
                        seq.stage("sync", s_x1spl, dma=True)

                    x13 = x1half[:, :].rearrange("p (tl d) -> p tl d", tl=8)
                    sqh3 = yvsb[:, 0:8192].rearrange("p (tl d) -> p tl d", tl=8)

                    def s_sqB(s):
                        return s.activation(yvsb[:, 0:8192], x1half[:, :], AF.Square)
                    seq.stage("scalar", s_sqB)

                    def s_redB(v):
                        v.tensor_reduce(stc16[:, 0:8], x13, AXX, ALU.add)
                        return v.tensor_reduce(stc16[:, 8:16], sqh3, AXX, ALU.add)
                    seq.stage("vector", s_redB)

                    def s_muB4(v):
                        v.tensor_scalar(stc16[:, 16:24], stc16[:, 0:8], fcn[:, 3:4],
                                        None, ALU.mult)
                        return v.tensor_scalar(stc16[:, 24:32], stc16[:, 8:16],
                                               fcn[:, 3:4], None, ALU.mult)
                    seq.stage("vector", s_muB4)

                    def s_msqB(s):
                        return s.activation(stc16[:, 48:56], stc16[:, 16:24], AF.Square)
                    seq.stage("scalar", s_msqB)

                    def s_varB(v):
                        return v.tensor_tensor(stc16[:, 56:64], stc16[:, 24:32],
                                               stc16[:, 48:56], ALU.subtract)
                    seq.stage("vector", s_varB)

                    def s_invB(s):
                        seq.scl_tick += 1
                        s.activation(stc16[:, 72:80], stc16[:, 56:64], AF.Ln,
                                     bias=fcn[:, 0:1]).then_inc(seq.sems["scl"], 1)
                        s.wait_ge(seq.sems["scl"], seq.scl_tick)
                        return s.activation(stc16[:, 64:72], stc16[:, 72:80], AF.Exp,
                                            scale=fcn[:, 1:2])
                    seq.stage("scalar", s_invB)

                    for tl in range(8):
                        tt = half * 8 + tl

                        def s_ap3(v, tt=tt, tl=tl):
                            return v.tensor_scalar(ln3t[tl % 2][:, :],
                                                   x1half[:, tl * 1024:(tl + 1) * 1024],
                                                   stc16[:, 16 + tl:17 + tl],
                                                   stc16[:, 64 + tl:65 + tl],
                                                   ALU.subtract, ALU.mult)
                        seq.stage("vector", s_ap3)

                        def s_t3dma(sy, sem, tt=tt, tl=tl):
                            for dk in range(8):
                                sy.dma_start(out=ln3T[:, dk * 2048 + tt * 128:
                                                      dk * 2048 + (tt + 1) * 128],
                                             in_=ln3t[tl % 2][:, dk * 128:(dk + 1) * 128],
                                             transpose=True).then_inc(sem, 16)
                        seq.stage("sync", s_t3dma, dma=True, ndma=8,
                                  signal="p4_done" if tt == 15 else None)

            with ExitStack() as p5:
                sb5 = lambda n, s, d: p5.enter_context(nc.sbuf_tensor(n, s, d))
                ps5 = lambda n, s: p5.enter_context(nc.psum_tensor(n, s, F32))
                gbuf = sb5("gbuf", [128, 16384], BF16)
                hbuf = sb5("hbuf", [128, 16384], BF16)
                wA = [sb5(f"wA{i}", [128, 4096], BF16) for i in range(2)]
                wB = [sb5(f"wB{i}", [128, 1024], BF16) for i in range(2)]
                g1sb = [sb5(f"g1sb{i}", [128, 512], BF16) for i in range(2)]
                t2f = [sb5(f"t2f{i}", [128, 512], F32) for i in range(2)]
                w3c = [sb5(f"w3c{i}", [128, 512], BF16) for i in range(2)]
                w3T = sb5("w3T", [128, 4096], BF16)
                x1tl = sb5("x1tl", [128, 1024], F32)
                otile = sb5("otile", [128, 1024], F32)
                pA = [ps5(f"pA{i}", [128, 512]) for i in range(2)]
                pB = [ps5(f"pB{i}", [128, 512]) for i in range(2)]

                def emit_output_part(ti):
                    for ts_ in range(4):
                        tt = ti * 4 + ts_

                        def s_x1l(sy, sem, tt=tt):
                            sy.dma_start(out=x1tl[:, :],
                                         in_=x1sp[:, tt * 1024:(tt + 1) * 1024]
                                         ).then_inc(sem, 16)
                        seq.stage("sync", s_x1l, dma=True,
                                  after=(f"t3{ti}_7",) if ts_ == 0 else ())

                        wt = w3T[:, ts_ * 1024:(ts_ + 1) * 1024]
                        ln_stats(wt)

                        def s_fin(v, wt=wt):
                            i1 = v.tensor_scalar(scr4[:, :], wt, stc[:, 2:3], stc[:, 6:7],
                                                 ALU.subtract, ALU.mult)
                            seq.tick(v, i1)
                            return v.tensor_tensor(otile[:, :], scr4[:, :], x1tl[:, :],
                                                   ALU.add)
                        seq.stage("vector", s_fin,
                                  signal=f"ofin{ti}" if ts_ == 3 else None)

                        def s_ost(sy, sem, tt=tt):
                            sy.dma_start(out=outp[tt * 128:(tt + 1) * 128, :],
                                         in_=otile[:, :]).then_inc(sem, 16)
                        seq.stage("sync", s_ost, dma=True)

                for ti in range(4):
                    tcol = ti * 512
                    for oj in range(32):
                        p = oj % 2

                        def s_wload(g, sem, oj=oj, p=p):
                            g.dma_start(out=wA[p][:, 0:1024],
                                        in_=w1b[oj, :, :]).then_inc(sem, 16)
                            g.dma_start(out=wB[p][:, :],
                                        in_=w2b[oj, :, :]).then_inc(sem, 16)
                        if oj >= 2:
                            aft = (f"wm{ti}_{oj - 2}",)
                        elif ti > 0:
                            aft = (f"w3m{ti - 1}_{6 + p}",)
                        else:
                            aft = ("p4_done", "cv_all")
                        seq.stage("gpsimd", s_wload, dma=True, ndma=2, nochain=True,
                                  signal=f"wl{ti}_{oj}", after=aft)

                        def s_wmm(t, oj=oj, p=p, tcol=tcol):
                            for dk in range(8):
                                st1 = wA[p][:, dk * 128:(dk + 1) * 128]
                                st2 = wB[p][:, dk * 128:(dk + 1) * 128]
                                mov = ln3T[:, dk * 2048 + tcol: dk * 2048 + tcol + 512]
                                t.matmul(pA[p][:, :], st1, mov,
                                         start=(dk == 0), stop=(dk == 7))
                                last = t.matmul(pB[p][:, :], st2, mov,
                                                start=(dk == 0), stop=(dk == 7))
                            return last
                        aft = [f"wl{ti}_{oj}"]
                        if oj >= 2:
                            aft.append(f"sg{ti}_{oj - 2}")
                        elif ti > 0:
                            aft += [f"e3{ti - 1}_{6 + p}", f"sg{ti - 1}_{30 + p}"]
                        else:
                            aft.append("p4_done")
                        seq.stage("tensor", s_wmm, nochain=True, after=tuple(aft),
                                  signal=f"wm{ti}_{oj}")

                        def s_silu(s, oj=oj, p=p):
                            return s.activation(g1sb[p][:, :], pA[p][:, :], AF.Silu,
                                                scale=colbuf[:, 128 + oj:129 + oj])
                        seq.stage("scalar", s_silu, nochain=True,
                                  after=(f"wm{ti}_{oj}",), signal=f"ss{ti}_{oj}")

                        def s_g(v, oj=oj, p=p):
                            i1 = v.tensor_scalar(t2f[p][:, :], pB[p][:, :],
                                                 colbuf[:, 160 + oj:161 + oj],
                                                 None, ALU.mult)
                            seq.tick(v, i1)
                            return v.tensor_tensor(gbuf[:, oj * 512:(oj + 1) * 512],
                                                   g1sb[p][:, :], t2f[p][:, :],
                                                   ALU.mult)
                        aft = [f"ss{ti}_{oj}"]
                        if oj >= 2:
                            aft.append(f"sg{ti}_{oj - 2}")
                        elif ti > 0:
                            aft.append(f"w3m{ti - 1}_7")
                        seq.stage("vector", s_g, nochain=True, after=tuple(aft),
                                  signal=f"sg{ti}_{oj}")

                    if ti > 0:
                        emit_output_part(ti - 1)

                    for (wb, scol, src, dst, ev, pv) in (
                            (h1b, 192, gbuf, hbuf, "h1", None),
                            (h2b, 224, hbuf, gbuf, "h2", "h1")):
                        for oj in range(32):
                            p = oj % 2

                            def s_hload(g, sem, oj=oj, p=p, wb=wb):
                                g.dma_start(out=wA[p][:, :],
                                            in_=wb[oj, :, :]).then_inc(sem, 16)
                            if oj >= 2:
                                aft = (f"{ev}m{ti}_{oj - 2}",)
                            elif ev == "h1":
                                aft = (f"wm{ti}_{30 + p}",)
                            else:
                                aft = (f"h1m{ti}_{30 + p}",)
                            seq.stage("gpsimd", s_hload, dma=True, ndma=1,
                                      nochain=True, signal=f"{ev}l{ti}_{oj}", after=aft)

                            def s_hmm(t, oj=oj, p=p, src=src):
                                for dk in range(32):
                                    stt = wA[p][:, dk * 128:(dk + 1) * 128]
                                    last = t.matmul(pA[p][:, :], stt,
                                                    src[:, dk * 512:(dk + 1) * 512],
                                                    start=(dk == 0), stop=(dk == 31))
                                return last
                            aft = [f"{ev}l{ti}_{oj}"]
                            if oj >= 2:
                                aft.append(f"ge{ev}{ti}_{oj - 2}")
                            elif ev == "h1":
                                aft += [f"ss{ti}_{30 + p}", f"sg{ti}_31"]
                            else:
                                aft.append(f"geh1{ti}_31")
                            seq.stage("tensor", s_hmm, nochain=True, after=tuple(aft),
                                      signal=f"{ev}m{ti}_{oj}")

                            def s_gelu(s, oj=oj, p=p, scol=scol, dst=dst):
                                return s.activation(dst[:, oj * 512:(oj + 1) * 512],
                                                    pA[p][:, :], AF.Gelu,
                                                    scale=colbuf[:, scol + oj:
                                                                 scol + oj + 1])
                            seq.stage("scalar", s_gelu, nochain=True,
                                      after=(f"{ev}m{ti}_{oj}",),
                                      signal=f"ge{ev}{ti}_{oj}")

                    for ej in range(8):
                        p = ej % 2

                        def s_3load(g, sem, ej=ej, p=p):
                            g.dma_start(out=wA[p][:, :],
                                        in_=w3b[ej, :, :]).then_inc(sem, 16)
                        aft = ((f"w3m{ti}_{ej - 2}",) if ej >= 2
                               else (f"h2m{ti}_{30 + p}",))
                        seq.stage("gpsimd", s_3load, dma=True, ndma=1, nochain=True,
                                  signal=f"w3l{ti}_{ej}", after=aft)

                        def s_3mm(t, ej=ej, p=p):
                            for dk in range(32):
                                stt = wA[p][:, dk * 128:(dk + 1) * 128]
                                last = t.matmul(pA[p][:, :], stt,
                                                gbuf[:, dk * 512:(dk + 1) * 512],
                                                start=(dk == 0), stop=(dk == 31))
                            return last
                        aft = [f"w3l{ti}_{ej}"]
                        if ej >= 2:
                            aft.append(f"e3{ti}_{ej - 2}")
                        else:
                            aft += [f"geh2{ti}_{30 + p}", f"geh2{ti}_31"]
                        seq.stage("tensor", s_3mm, nochain=True, after=tuple(aft),
                                  signal=f"w3m{ti}_{ej}")

                        def s_3evac(v, ej=ej, p=p):
                            return v.tensor_scalar(w3c[p][:, :], pA[p][:, :],
                                                   colb2[:, ej:ej + 1], None, ALU.mult)
                        aft = [f"w3m{ti}_{ej}"]
                        if ej >= 2:
                            aft.append(f"t3{ti}_{ej - 2}")
                        seq.stage("vector", s_3evac, nochain=True, after=tuple(aft),
                                  signal=f"e3{ti}_{ej}")

                        def s_3tdma(sy, sem, ej=ej, p=p):
                            for ts_ in range(4):
                                sy.dma_start(out=w3T[:, ts_ * 1024 + ej * 128:
                                                     ts_ * 1024 + (ej + 1) * 128],
                                             in_=w3c[p][:, ts_ * 128:(ts_ + 1) * 128],
                                             transpose=True).then_inc(sem, 16)
                        aft = [f"e3{ti}_{ej}"]
                        if ej == 0 and ti > 0:
                            aft.append(f"ofin{ti - 1}")
                        seq.stage("sync", s_3tdma, dma=True, ndma=4, nochain=True,
                                  after=tuple(aft), signal=f"t3{ti}_{ej}")



                emit_output_part(3)

        def s_done(g):
            return g.engine_nop()
        seq.stage("gpsimd", s_done)

        if True:
            for e in seq.engines + ["coll", "scl", "gpc"]:
                nc.sync.sem_clear(seq.sems[e])
            nc.all_engine_barrier()
            with nc.Block() as block:
                seq.build(block)
    return nc


_CACHE = {}


def kernel(L0_attn_u, L0_attn_in_w, L0_attn_in_b, L0_attn_short_w, L0_attn_short_b,
           L0_attn_mlp_w1, L0_attn_mlp_b1, L0_attn_mlp_w2, L0_attn_mlp_b2,
           L0_attn_mlp_w3, L0_attn_mlp_b3, L0_attn_mlp_wo, L0_attn_freq,
           L0_attn_filt_bias, L0_attn_out_w, L0_attn_out_b,
           L0_ffn_w1_w, L0_ffn_w1_s, L0_ffn_w2_w, L0_ffn_w2_s,
           L0_ffn_h1_w, L0_ffn_h1_s, L0_ffn_h2_w, L0_ffn_h2_s,
           L0_ffn_w3_w, L0_ffn_w3_s):
    f32 = lambda a: np.ascontiguousarray(np.asarray(a), dtype=np.float32)
    u = f32(L0_attn_u)
    uflat = u.reshape(16384, D)

    chunkcol = lambda vec, n: np.ascontiguousarray(
        np.asarray(vec, np.float32).reshape(n, 128).T)

    def pack(wT, dk, oj):
        i_, o_ = wT.shape
        return np.ascontiguousarray(
            wT.reshape(dk, 128, oj, 128).transpose(2, 1, 0, 3).reshape(oj, 128, dk * 128))

    shared = {
        "inwT": pack(f32(L0_attn_in_w), 8, 24),
        "inb_c": chunkcol(L0_attn_in_b, 24),
        "shb_c": chunkcol(L0_attn_short_b, 24),
        "sw_c": np.ascontiguousarray(np.concatenate(
            [chunkcol(np.asarray(L0_attn_short_w, np.float32)[:, k], 24)
             for k in range(3)], axis=1)),
        "outwT": f32(L0_attn_out_w),
        "outb_c": chunkcol(L0_attn_out_b, 8),
        "w1_t": pack(f32(L0_ffn_w1_w).T, 8, 32),
        "w2_t": pack(f32(L0_ffn_w2_w).T, 8, 32),
        "h1_t": pack(f32(L0_ffn_h1_w).T, 32, 32),
        "h2_t": pack(f32(L0_ffn_h2_w).T, 32, 32),
        "w3_t": pack(f32(L0_ffn_w3_w).T, 32, 8),
        "s1_c": chunkcol(L0_ffn_w1_s, 32),
        "s2_c": chunkcol(L0_ffn_w2_s, 32),
        "sh1_c": chunkcol(L0_ffn_h1_s, 32),
        "sh2_c": chunkcol(L0_ffn_h2_s, 32),
        "s3_c": chunkcol(L0_ffn_w3_s, 8),
        "mlpw1": f32(L0_attn_mlp_w1),
        "mlpw2": f32(L0_attn_mlp_w2),
        "mlpw3": f32(L0_attn_mlp_w3),
        "freq_c": np.ascontiguousarray(f32(L0_attn_freq).reshape(64, 1)),
        "b1_c": np.ascontiguousarray(f32(L0_attn_mlp_b1).reshape(64, 1)),
        "b2_c": np.ascontiguousarray(f32(L0_attn_mlp_b2).reshape(64, 1)),
        "b3_c": np.ascontiguousarray(f32(L0_attn_mlp_b3).reshape(64, 1)),
        "zT": _z_const(),
        "fconst": np.ascontiguousarray(np.tile(np.array(
            [EPS, -0.5, -PI, 1.0 / D, 65.0 * PI, 2.0 * PI, 1.0 / (2.0 * PI), -2.0 * PI],
            np.float32)[None, :], (128, 1))),
    }
    shared.update(_conv_constants())

    wo = f32(L0_attn_mlp_wo)
    fbias = f32(L0_attn_filt_bias)

    in_maps = []
    for r in range(NCORE):
        half = r % 2
        u_s = np.zeros((TPAD, D), np.float32)
        u_s[2:2 + TOK] = uflat[r * TOK:(r + 1) * TOK]
        if half == 1:
            u_s[0:2] = uflat[r * TOK - 2: r * TOK]
        mask = (np.zeros((128, 2), np.float32) if half == 0
                else np.ones((128, 2), np.float32))
        fb_s = fbias[128 * r:128 * r + 128]
        m = dict(shared)
        idxg = np.zeros((128, 8), np.int32)
        for j in range(8):
            idxg[:, j] = 1024 * j + 128 * r + np.arange(128)
        idx2g = np.zeros((128, 8), np.int32)
        base = (0 if r < 4 else 4096) + (r % 4) * 128
        for ck in range(8):
            idx2g[:, ck] = base + ck * 512 + np.arange(128)
        m.update({
            "u_s": u_s,
            "idx_g": idxg,
            "idx2_g": idx2g,
            "mask2": mask,
            "wo_s": np.ascontiguousarray(wo[:, 128 * r:128 * r + 128]),
            "decay_s": _decay_slice(r),
            "fbrep": np.ascontiguousarray(
                np.tile(np.tile(fb_s, 4)[None, :], (128, 1)).astype(np.float32)),
        })
        in_maps.append(m)

    if "nc" not in _CACHE:
        _CACHE["nc"] = build_nc()
    res = run_bass_kernel_spmd(_CACHE["nc"], in_maps, core_ids=list(range(NCORE)))
    out = np.concatenate([np.asarray(res.results[r]["outp"], dtype=np.float32)
                          for r in range(NCORE)], axis=0)
    return out.reshape(4, 4096, 1024)


# version 20 (final)
# speedup vs baseline: 1.0183x; 1.0183x over previous
import math
import numpy as np
import concourse.bass as bass
import concourse.mybir as mybir
from concourse.bass_utils import run_bass_kernel_spmd
from contextlib import ExitStack

F32 = mybir.dt.float32
BF16 = mybir.dt.bfloat16
AF = mybir.ActivationFunctionType
ALU = mybir.AluOpType
AXX = mybir.AxisListType.X

D = 1024
INNER = 3072
DH = 4096
L = 4096
TOK = 2048
TPAD = 2176
NCORE = 8
EPS = 1e-5
PI = math.pi


class Seq:

    def __init__(self, nc):
        self.nc = nc
        self.engines = ["tensor", "vector", "scalar", "sync", "gpsimd"]
        self.sems = {}
        self.stages = []
        self.events = {}

    def alloc_sems(self, ctx):
        for e in self.engines + ["coll", "scl", "gpc", "vtk"]:
            self.sems[e] = ctx.enter_context(self.nc.semaphore(f"seq_{e}"))
        self.scl_tick = 0
        self.ticks = {"scl": 0, "vtk": 0}

    def tick(self, eng, ins, sem="vtk"):
        self.ticks[sem] += 1
        ins.then_inc(self.sems[sem], 1)
        eng.wait_ge(self.sems[sem], self.ticks[sem])

    def stage(self, engine, fn, dma=False, ndma=1, signal=None, after=(), nochain=False,
              coll=False):
        self.stages.append(dict(engine=engine, fn=fn, inc=16 * ndma if dma else 1,
                                dma=dma, signal=signal, after=tuple(after),
                                nochain=nochain, coll=coll))

    def build(self, block):
        per_engine = {e: [] for e in self.engines}
        counts = {e: 0 for e in self.engines}
        prev = None
        counts["coll"] = 0
        counts["gpc"] = 0
        prev_engine = None
        prev_same = {}
        for st in self.stages:
            e = st["engine"]
            waits = []
            if (not st["nochain"]) and (not st["coll"]) and prev is not None:
                waits.append(prev)
            if e in ("vector", "scalar") and e in prev_same:
                waits.append(prev_same[e])
            for name in st["after"]:
                waits.append(self.events[name])
            if st["coll"]:
                counts["coll"] += 1
                if st["signal"]:
                    self.events[st["signal"]] = ("coll", counts["coll"])
                per_engine[e].append((waits, st["fn"], 1, "coll", counts["coll"]))
                continue
            sname = "gpc" if (e == "gpsimd" and not st["dma"]) else e
            counts[sname] += st["inc"]
            if st["signal"]:
                self.events[st["signal"]] = (sname, counts[sname])
            per_engine[e].append((waits, st["fn"], st["inc"],
                                  st["dma"] or sname, counts[sname]))
            prev_same[e] = (sname, counts[sname])
            if not st["nochain"]:
                prev = (sname, counts[sname])
                prev_engine = e
        self.final = dict(counts)

        def make_body(engine):
            items = per_engine[engine]

            def body(eng):
                seen = {}
                for waits, fn, inc, isdma, post in items:
                    for (we, wv) in waits:
                        if seen.get(we, -1) < wv:
                            eng.wait_ge(self.sems[we], wv)
                            seen[we] = wv
                    if isdma == "coll":
                        last = fn(eng)
                        assert last is not None, f"coll stage on {engine} returned None"
                        last.then_inc(self.sems["coll"], inc)
                    elif isdma is True:
                        fn(eng, self.sems[engine])
                        eng.wait_ge(self.sems[engine], post)
                        seen[engine] = max(seen.get(engine, -1), post)
                    else:
                        sname = isdma
                        last = fn(eng)
                        assert last is not None, f"stage on {engine} returned None"
                        last.then_inc(self.sems[sname], inc)
            return body

        for e in self.engines:
            if per_engine[e]:
                getattr(block, e)(make_body(e))


def _conv_constants():
    n2 = np.arange(64, dtype=np.float64)[:, None]
    k2 = np.arange(128, dtype=np.float64)[None, :]
    th = 2 * np.pi * n2 * k2 / 128.0
    CS = np.concatenate([np.cos(th), -np.sin(th)], axis=1)
    CS = np.concatenate([CS, CS], axis=0)

    n1 = np.arange(64, dtype=np.float64)[:, None]
    th = 2 * np.pi * n1 * k2 / 8192.0
    Trep_re = np.tile(np.concatenate([np.cos(th)] * 2, axis=0), (1, 4))
    Trep_im = np.tile(np.concatenate([-np.sin(th)] * 2, axis=0), (1, 4))

    k1 = np.arange(64, dtype=np.float64)[None, :]
    th = 2 * np.pi * n1 * k1 / 64.0
    Wr = np.concatenate([np.cos(th)] * 2, axis=0)
    Wi = np.concatenate([-np.sin(th)] * 2, axis=0)

    k1c = np.arange(64, dtype=np.float64)[:, None]
    n1r = np.arange(64, dtype=np.float64)[None, :]
    th = 2 * np.pi * k1c * n1r / 64.0
    Vr, Vi = np.cos(th), np.sin(th)
    movP = np.concatenate([np.concatenate([Vr, Vi], axis=1)] * 2, axis=0)
    movM = np.concatenate([np.concatenate([-Vi, Vr], axis=1)] * 2, axis=0)

    k2c = np.arange(128, dtype=np.float64)[:, None]
    n2r = np.arange(64, dtype=np.float64)[None, :]
    U = np.zeros((128, 8192), dtype=np.float64)
    for i1 in range(64):
        th = 2 * np.pi * (i1 + 64 * n2r) * k2c / 8192.0
        U[:, i1 * 128: i1 * 128 + 64] = np.cos(th) / 8192.0
        U[:, i1 * 128 + 64: i1 * 128 + 128] = -np.sin(th) / 8192.0

    f32 = lambda a: np.ascontiguousarray(a, dtype=np.float32)
    return dict(c_CS=f32(CS), c_Tre=f32(Trep_re), c_Tim=f32(Trep_im),
                c_Wr=f32(Wr), c_Wi=f32(Wi), c_WiNeg=f32(-Wi),
                c_movP=f32(movP), c_movM=f32(movM), c_Ucons=f32(U))


def _z_const():
    t = np.linspace(0.0, 1.0, L)
    tr = np.linspace(0.0, L - 1.0, L)
    w = 2.0 * np.pi * tr / L
    z = np.stack([t, np.cos(1e-4 * w), -np.sin(1e-4 * w)], axis=0)
    return np.ascontiguousarray(z, dtype=np.float32)


def _decay_slice(core):
    max_d = np.log(0.01) / 0.3
    min_d = np.log(0.01) / 1.5
    deltas = np.abs(np.linspace(min_d, max_d, D))[128 * core: 128 * core + 128]
    t = np.linspace(0.0, 1.0, L)
    dec = np.exp(-t[None, :] * deltas[:, None])
    return np.ascontiguousarray(dec, dtype=np.float32)


def build_nc():
    nc = bass.Bass("TRN2", target_bir_lowering=False)
    ap = {}

    def din(name, shape):
        ap[name] = nc.dram_tensor(name, list(shape), F32, kind="ExternalInput").ap()

    din("u_s", (TPAD, D)); din("mask2", (128, 2))
    din("inwT", (24, 128, 1024)); din("inb_c", (128, 24))
    din("sw_c", (128, 72)); din("shb_c", (128, 24))
    din("outwT", (D, D)); din("outb_c", (128, 8))
    din("w1_t", (32, 128, 8 * 128)); din("w2_t", (32, 128, 8 * 128))
    din("h1_t", (32, 128, DH)); din("h2_t", (32, 128, DH))
    din("w3_t", (8, 128, DH))
    din("s1_c", (128, 32)); din("s2_c", (128, 32))
    din("sh1_c", (128, 32)); din("sh2_c", (128, 32)); din("s3_c", (128, 8))
    din("mlpw1", (3, 64)); din("mlpw2", (64, 64)); din("mlpw3", (64, 64))
    din("wo_s", (64, 128)); din("freq_c", (64, 1))
    din("b1_c", (64, 1)); din("b2_c", (64, 1)); din("b3_c", (64, 1))
    din("zT", (3, L)); din("decay_s", (128, L)); din("fbrep", (128, 512))
    din("fconst", (128, 8))
    for k, v in _conv_constants().items():
        din(k, v.shape)

    a1in = nc.dram_tensor("a1in", [1024, 2048], BF16).ap()
    a1out = nc.dram_tensor("a1out", [8192, 2048], BF16).ap()
    myvx = nc.dram_tensor("myvx", [8, 128, 2048], BF16).ap()
    a2in = nc.dram_tensor("a2in", [8, 128, 2048], BF16).ap()
    a2out = nc.dram_tensor("a2out", [8192, 2048], BF16).ap()
    ap["idx_g"] = nc.dram_tensor("idx_g", [128, 8], mybir.dt.int32,
                                 kind="ExternalInput").ap()
    ap["idx2_g"] = nc.dram_tensor("idx2_g", [128, 8], mybir.dt.int32,
                                  kind="ExternalInput").ap()
    x0sp = nc.dram_tensor("x0sp", [128, 16384], BF16).ap()
    x1sp = nc.dram_tensor("x1sp", [128, 16384], F32).ap()
    kscr = nc.dram_tensor("kscr", [128, L], BF16).ap()
    outp = nc.dram_tensor("outp", [TOK, D], F32, kind="ExternalOutput").ap()
    w1b = nc.dram_tensor("w1b", [32, 128, 1024], BF16).ap()
    w2b = nc.dram_tensor("w2b", [32, 128, 1024], BF16).ap()
    h1b = nc.dram_tensor("h1b", [32, 128, 4096], BF16).ap()
    h2b = nc.dram_tensor("h2b", [32, 128, 4096], BF16).ap()
    w3b = nc.dram_tensor("w3b", [8, 128, 4096], BF16).ap()

    with ExitStack() as ctx:
        sbt = lambda n, s, d: ctx.enter_context(nc.sbuf_tensor(n, s, d))

        seq = Seq(nc)
        seq.alloc_sems(ctx)

        cv_tasks = []
        for oj in range(0, 32, 8):
            cv_tasks.append((w1b[oj:oj + 8], ap["w1_t"][oj:oj + 8]))
            cv_tasks.append((w2b[oj:oj + 8], ap["w2_t"][oj:oj + 8]))
        for oj in range(32):
            cv_tasks.append((h1b[oj], ap["h1_t"][oj]))
        for oj in range(32):
            cv_tasks.append((h2b[oj], ap["h2_t"][oj]))
        for ej in range(0, 8, 2):
            cv_tasks.append((w3b[ej:ej + 2], ap["w3_t"][ej:ej + 2]))
        cv_state = {"i": 0}

        def emit_cv():
            i = cv_state["i"]
            if i >= len(cv_tasks):
                return
            cv_state["i"] = i + 1
            dst, src = cv_tasks[i]

            def s_cv(g, sem, dst=dst, src=src):
                g.dma_start(out=dst, in_=src).then_inc(sem, 16)
            seq.stage("gpsimd", s_cv, dma=True, ndma=1, nochain=True,
                      signal="cv_all" if i == len(cv_tasks) - 1 else None)

        CSc = sbt("CSc", [128, 256], BF16)
        Tre = sbt("Tre", [128, 512], BF16); Tim = sbt("Tim", [128, 512], BF16)
        Wrc = sbt("Wrc", [128, 64], BF16); Wic = sbt("Wic", [128, 64], BF16)
        WiN = sbt("WiN", [128, 64], BF16)
        mvP = sbt("mvP", [128, 128], BF16); mvM = sbt("mvM", [128, 128], BF16)
        Ucn = sbt("Ucn", [128, 8192], BF16)
        fbr = sbt("fbr", [128, 512], BF16)
        colbuf = sbt("colbuf", [128, 256], F32)
        colb2 = sbt("colb2", [128, 16], F32)
        mcol = sbt("mcol", [64, 8], F32)
        masks = sbt("masks", [128, 2], BF16)
        scr4 = sbt("scr4", [128, 1024], F32)
        stc = sbt("stc", [128, 16], F32)
        fcn = sbt("fcn", [128, 8], F32)
        idxs = sbt("idxs", [128, 8], mybir.dt.int32)
        idxs2 = sbt("idxs2", [128, 8], mybir.dt.int32)

        def s_const_loads(g, sem):
            pairs = [("c_CS", CSc), ("c_Tre", Tre), ("c_Tim", Tim),
                     ("c_Wr", Wrc), ("c_Wi", Wic), ("c_WiNeg", WiN),
                     ("c_movP", mvP), ("c_movM", mvM), ("c_Ucons", Ucn),
                     ("fbrep", fbr), ("mask2", masks)]
            for name, dst in pairs:
                g.dma_start(out=dst[:, :], in_=ap[name][:, :]).then_inc(sem, 16)
            for dst_sl, name in [(colbuf[:, 0:24], "inb_c"), (colbuf[:, 24:48], "shb_c"),
                                 (colbuf[:, 48:120], "sw_c"), (colbuf[:, 120:128], "outb_c"),
                                 (colbuf[:, 128:160], "s1_c"), (colbuf[:, 160:192], "s2_c"),
                                 (colbuf[:, 192:224], "sh1_c"), (colbuf[:, 224:256], "sh2_c"),
                                 (colb2[:, 0:8], "s3_c"), (mcol[:, 0:1], "freq_c"),
                                 (mcol[:, 1:2], "b1_c"), (mcol[:, 2:3], "b2_c"),
                                 (mcol[:, 3:4], "b3_c"), (fcn[:, :], "fconst"),
                                 (idxs[:, :], "idx_g"), (idxs2[:, :], "idx2_g")]:
                g.dma_start(out=dst_sl, in_=ap[name][:, :]).then_inc(sem, 16)
        seq.stage("gpsimd", s_const_loads, dma=True, ndma=27)

        def ln_stats(buf_ap):
            def s_red(v):
                return v.tensor_reduce(stc[:, 0:1], buf_ap, AXX, ALU.add)
            seq.stage("vector", s_red)

            def s_sq(s):
                return s.activation(scr4[:, :], buf_ap, AF.Square,
                                    accum_out=stc[:, 1:2])
            seq.stage("scalar", s_sq)

            def s_muA(v):
                v.tensor_scalar(stc[:, 2:3], stc[:, 0:1], fcn[:, 3:4], None, ALU.mult)
                return v.tensor_scalar(stc[:, 4:5], stc[:, 1:2], fcn[:, 3:4],
                                       None, ALU.mult)
            seq.stage("vector", s_muA)

            def s_muB(s):
                return s.activation(stc[:, 3:4], stc[:, 2:3], AF.Square)
            seq.stage("scalar", s_muB)

            def s_muC(v):
                return v.tensor_tensor(stc[:, 4:5], stc[:, 4:5], stc[:, 3:4],
                                       ALU.subtract)
            seq.stage("vector", s_muC)

            def s_inv(s):
                seq.scl_tick += 1
                s.activation(stc[:, 5:6], stc[:, 4:5], AF.Ln,
                             bias=fcn[:, 0:1]).then_inc(seq.sems["scl"], 1)
                s.wait_ge(seq.sems["scl"], seq.scl_tick)
                return s.activation(stc[:, 6:7], stc[:, 5:6], AF.Exp,
                                    scale=fcn[:, 1:2])
            seq.stage("scalar", s_inv)

        with ExitStack() as p1:
            sb1 = lambda n, s, d: p1.enter_context(nc.sbuf_tensor(n, s, d))
            ps1 = lambda n, s: p1.enter_context(nc.psum_tensor(n, s, F32))
            ln1T = sb1("ln1T", [128, 8 * TPAD], BF16)
            inwb = [sb1(f"inwb{i}", [128, 1024], BF16) for i in range(2)]
            utile = sb1("utile", [128, 1024], F32)
            ln1t = sb1("ln1t", [128, 1024], BF16)
            upb = [sb1(f"upb{i}", [128, TPAD], BF16) for i in range(2)]
            tap1 = sb1("tap1", [128, 2048], BF16)
            tap2 = sb1("tap2", [128, 2048], BF16)
            x1sb = sb1("x1sb", [128, 8 * 2048], BF16)
            x0c = [sb1(f"x0c{i}", [128, 2048], BF16) for i in range(2)]
            vxc = [sb1(f"vxc{i}", [128, 2048], BF16) for i in range(2)]
            zTb = sb1("zTb", [3, L], BF16)
            mw1 = sb1("mw1", [3, 64], BF16)
            mw2 = sb1("mw2", [64, 64], BF16); mw3 = sb1("mw3", [64, 64], BF16)
            wos = sb1("wos", [64, 128], BF16)
            dcy = sb1("dcy", [128, L], BF16)
            harg = sb1("harg", [64, 512], F32)
            hri = sb1("hri", [64, 512], mybir.dt.int32)
            hrf = sb1("hrf", [64, 512], F32)
            hsb = sb1("hsb", [64, 512], BF16)
            ksb = sb1("ksb", [128, 512], BF16)
            ups_m = ps1("ups_m", [128, 2048])
            ups_t = ps1("ups_t", [128, 128])
            hps = ps1("hps", [64, 512])
            kps = ps1("kps", [128, 512])

            for tt in range(17):
                def s_uload(sy, sem, tt=tt):
                    sy.dma_start(out=utile[:, :],
                                 in_=ap["u_s"][tt * 128:(tt + 1) * 128, :]
                                 ).then_inc(sem, 16)
                seq.stage("sync", s_uload, dma=True)

                ln_stats(utile[:, :])

                def s_apply(v):
                    return v.tensor_scalar(ln1t[:, :], utile[:, :], stc[:, 2:3],
                                           stc[:, 6:7], ALU.subtract, ALU.mult)
                seq.stage("vector", s_apply)

                def s_tdma(sy, sem, tt=tt):
                    for dk in range(8):
                        sy.dma_start(out=ln1T[:, dk * TPAD + tt * 128:
                                              dk * TPAD + (tt + 1) * 128],
                                     in_=ln1t[:, dk * 128:(dk + 1) * 128],
                                     transpose=True).then_inc(sem, 16)
                seq.stage("sync", s_tdma, dma=True, ndma=8,
                          signal="ln1T_ok" if tt == 16 else None)

            cj_order = list(range(8, 24)) + list(range(0, 8))
            for ci, cj in enumerate(cj_order):
                kind = "x1" if cj < 16 and cj >= 8 else ("v" if cj >= 16 else "x0")

                def s_iwl(g, sem, cj=cj, ci=ci):
                    g.dma_start(out=inwb[ci % 2][:, :],
                                in_=ap["inwT"][cj, :, :]).then_inc(sem, 16)
                seq.stage("gpsimd", s_iwl, dma=True, ndma=1, nochain=True,
                          signal=f"iwl{ci}",
                          after=(f"iwm{ci - 2}",) if ci >= 2 else ())

                def s_mm(t, cj=cj, ci=ci):
                    for dk in range(8):
                        stt = inwb[ci % 2][:, dk * 128:(dk + 1) * 128]
                        for ni in range(4):
                            t.matmul(ups_m[:, ni * 512:(ni + 1) * 512], stt,
                                     ln1T[:, dk * TPAD + ni * 512:
                                          dk * TPAD + (ni + 1) * 512],
                                     start=(dk == 0), stop=(dk == 7))
                        last = t.matmul(ups_t[:, :], stt,
                                        ln1T[:, dk * TPAD + 2048: dk * TPAD + 2176],
                                        start=(dk == 0), stop=(dk == 7))
                    return last
                aft = [f"iwl{ci}"]
                aft.append("ln1T_ok" if ci == 0 else f"sevac{ci - 1}")
                seq.stage("tensor", s_mm, nochain=True, after=tuple(aft),
                          signal=f"iwm{ci}")

                def s_evac(s, cj=cj, ci=ci):
                    ub = upb[ci % 2]
                    s.activation(ub[:, 0:2048], ups_m[:, :], AF.Identity,
                                 bias=colbuf[:, cj:cj + 1])
                    return s.activation(ub[:, 2048:2176], ups_t[:, :], AF.Identity,
                                        bias=colbuf[:, cj:cj + 1])
                aft = [f"iwm{ci}"] + ([f"staps{ci - 2}"] if ci >= 2 else [])
                seq.stage("scalar", s_evac, nochain=True, after=tuple(aft),
                          signal=f"sevac{ci}")

                def s_taps(v, cj=cj, ci=ci, kind=kind):
                    ub = upb[ci % 2]
                    i0 = v.tensor_tensor(ub[:, 0:2], ub[:, 0:2], masks[:, :],
                                         ALU.mult)
                    seq.tick(v, i0)
                    i1 = v.tensor_scalar(tap1[:, :], ub[:, 0:2048],
                                         colbuf[:, 48 + cj:49 + cj],
                                         colbuf[:, 24 + cj:25 + cj], ALU.mult, ALU.add)
                    seq.tick(v, i1)
                    i2 = v.scalar_tensor_tensor(tap2[:, :], ub[:, 1:2049],
                                                colbuf[:, 72 + cj:73 + cj], tap1[:, :],
                                                ALU.mult, ALU.add)
                    seq.tick(v, i2)
                    dst = (x1sb[:, (cj - 8) * 2048:(cj - 7) * 2048] if kind == "x1"
                           else x0c[ci % 2][:, :] if kind == "x0" else tap1[:, :])
                    last = v.scalar_tensor_tensor(dst, ub[:, 2:2050],
                                                  colbuf[:, 96 + cj:97 + cj],
                                                  tap2[:, :], ALU.mult, ALU.add)
                    if kind == "v":
                        seq.tick(v, last)
                        last = v.tensor_tensor(
                            vxc[ci % 2][:, :], tap1[:, :],
                            x1sb[:, (cj - 16) * 2048:(cj - 15) * 2048], ALU.mult)
                    return last
                aft = [f"sevac{ci}"]
                if ci >= 10:
                    aft.append("a1in_ok" if ci - 2 == 15 else f"cst{ci - 2}")
                seq.stage("vector", s_taps, nochain=True, after=tuple(aft),
                          signal=f"staps{ci}")

                if kind == "v":
                    def s_vx_store(sy, sem, cj=cj, ci=ci):
                        sy.dma_start(out=a1in[(cj - 16) * 128:(cj - 15) * 128, :],
                                     in_=vxc[ci % 2][:, :]).then_inc(sem, 16)
                    seq.stage("sync", s_vx_store, dma=True, nochain=True,
                              after=(f"staps{ci}",),
                              signal="a1in_ok" if cj == 23 else f"cst{ci}")
                if kind == "x0":
                    def s_x0_store(sy, sem, cj=cj, ci=ci):
                        sy.dma_start(out=x0sp[:, cj * 2048:(cj + 1) * 2048],
                                     in_=x0c[ci % 2][:, :]).then_inc(sem, 16)
                    seq.stage("sync", s_x0_store, dma=True, nochain=True,
                              after=(f"staps{ci}",), signal=f"cst{ci}")

                if cj == 23:
                    def s_a2a1(g):
                        return g.collective_compute(
                            "AllGather", ALU.bypass,
                            replica_groups=[list(range(NCORE))],
                            ins=[a1in[:, :].opt()], outs=[a1out[:, :].opt()])
                    seq.stage("gpsimd", s_a2a1, signal="a2a1_done", coll=True,
                              after=("a1in_ok",))

            def s_kload(g, sem):
                for name, dst in [("zT", zTb), ("mlpw1", mw1), ("mlpw2", mw2),
                                  ("mlpw3", mw3), ("wo_s", wos), ("decay_s", dcy)]:
                    g.dma_start(out=dst[:, :], in_=ap[name][:, :]).then_inc(sem, 16)
            seq.stage("gpsimd", s_kload, dma=True, ndma=6)

            def s_fb(v):
                v.tensor_tensor(mcol[:, 4:5], mcol[:, 0:1], mcol[:, 1:2], ALU.mult)
                v.tensor_tensor(mcol[:, 5:6], mcol[:, 0:1], mcol[:, 2:3], ALU.mult)
                return v.tensor_tensor(mcol[:, 6:7], mcol[:, 0:1], mcol[:, 3:4], ALU.mult)
            seq.stage("vector", s_fb)

            for tc in range(8):
                def s_l1(t, tc=tc):
                    return t.matmul(hps[:, :], mw1[:, :],
                                    zTb[:, tc * 512:(tc + 1) * 512],
                                    start=True, stop=True)
                seq.stage("tensor", s_l1)
                for li in range(3):
                    if li > 0:
                        wmat = mw2 if li == 1 else mw3

                        def s_lx(t, wmat=wmat):
                            return t.matmul(hps[:, :], wmat[:, :], hsb[:, :],
                                            start=True, stop=True)
                        seq.stage("tensor", s_lx)
                    fbcol = 4 + li

                    def s_arg(v, fbcol=fbcol):
                        i1 = v.tensor_scalar(harg[:, :], hps[:, :], mcol[:, 0:1],
                                             mcol[:, fbcol:fbcol + 1], ALU.mult, ALU.add)
                        seq.tick(v, i1)
                        i2 = v.tensor_scalar(hrf[:, :], harg[:, :], fcn[0:64, 6:7],
                                             None, ALU.mult)
                        seq.tick(v, i2)
                        i3 = v.tensor_copy(hri[:, :], hrf[:, :])
                        seq.tick(v, i3)
                        i4 = v.tensor_copy(hrf[:, :], hri[:, :])
                        seq.tick(v, i4)
                        return v.scalar_tensor_tensor(harg[:, :], hrf[:, :],
                                                      fcn[0:64, 7:8], harg[:, :],
                                                      ALU.mult, ALU.add)
                    seq.stage("vector", s_arg)

                    def s_sin(s):
                        return s.activation(hsb[:, :], harg[:, :], AF.Sin)
                    seq.stage("scalar", s_sin)

                def s_wo(t):
                    return t.matmul(kps[:, :], wos[:, :], hsb[:, :],
                                    start=True, stop=True)
                seq.stage("tensor", s_wo)

                def s_kmul(v, tc=tc):
                    return v.tensor_tensor(ksb[:, :], kps[:, :],
                                           dcy[:, tc * 512:(tc + 1) * 512], ALU.mult)
                seq.stage("vector", s_kmul)

                def s_kst(sy, sem, tc=tc):
                    sy.dma_start(out=kscr[:, tc * 512:(tc + 1) * 512],
                                 in_=ksb[:, :]).then_inc(sem, 16)
                seq.stage("sync", s_kst, dma=True)

        with ExitStack() as p3:
            sb3 = lambda n, s, d: p3.enter_context(nc.sbuf_tensor(n, s, d))
            ps3 = lambda n, s: p3.enter_context(nc.psum_tensor(n, s, F32))
            Kre = sb3("Kre", [128, 16384], BF16)
            Kim = sb3("Kim", [128, 16384], BF16)
            Draw = sb3("Draw", [128, 32768], BF16)
            Are = sb3("Are", [128, 512], BF16); Aim = sb3("Aim", [128, 512], BF16)
            Apre = sb3("Apre", [128, 512], BF16); Apim = sb3("Apim", [128, 512], BF16)
            tv1 = sb3("tv1", [128, 512], F32); tv2 = sb3("tv2", [128, 512], F32)
            tv3 = sb3("tv3", [128, 512], F32); tv4 = sb3("tv4", [128, 512], F32)
            Xre_sb = sb3("Xre_sb", [128, 512], BF16)
            Xim_sb = sb3("Xim_sb", [128, 512], BF16)
            Yre = sb3("Yre", [128, 512], BF16); Yim = sb3("Yim", [128, 512], BF16)
            gxa = sb3("gxa", [128, 2048], BF16)
            A_ps = ps3("A_ps", [128, 1024])
            Xre_ps = ps3("Xre_ps", [128, 512]); Xim_ps = ps3("Xim_ps", [128, 512])
            D_ps = ps3("D_ps", [128, 1024]); y_ps = ps3("y_ps", [128, 512])

            def fwd_group(xbuf, g, fg, first_sig, kside):
                def s_step1(t, g=g):
                    for cl in range(4):
                        cs = (4 * g + cl) * 64
                        t.matmul(A_ps[0:64, cl * 256:(cl + 1) * 256],
                                 xbuf[0:64, cs:cs + 64], CSc[0:64, :],
                                 start=True, stop=True, tile_position=(0, 0))
                        last = t.matmul(A_ps[64:128, cl * 256:(cl + 1) * 256],
                                        xbuf[64:128, cs:cs + 64], CSc[64:128, :],
                                        start=True, stop=True, tile_position=(64, 64))
                    return last
                aft = [] if fg == 0 else [f"tw_{fg - 1}"]
                if first_sig:
                    aft.append(first_sig)
                seq.stage("tensor", s_step1, nochain=True, after=tuple(aft),
                          signal=f"st1_{fg}")

                def s_tw(v):
                    apv = A_ps[:, :].rearrange("p (c r k) -> p c r k", c=4, r=2, k=128)
                    Apr = apv[:, :, 0, :]
                    Api = apv[:, :, 1, :]
                    t1v = tv1[:, :].rearrange("p (c k) -> p c k", c=4)
                    t2v = tv2[:, :].rearrange("p (c k) -> p c k", c=4)
                    t3v = tv3[:, :].rearrange("p (c k) -> p c k", c=4)
                    t4v = tv4[:, :].rearrange("p (c k) -> p c k", c=4)
                    Trev = Tre[:, :].rearrange("p (c k) -> p c k", c=4)
                    Timv = Tim[:, :].rearrange("p (c k) -> p c k", c=4)
                    v.tensor_tensor(t1v, Apr, Trev, ALU.mult)
                    v.tensor_tensor(t2v, Api, Timv, ALU.mult)
                    v.tensor_tensor(t3v, Apr, Timv, ALU.mult)
                    i4 = v.tensor_tensor(t4v, Api, Trev, ALU.mult)
                    seq.tick(v, i4)
                    v.tensor_tensor(Apre[:, :], tv1[:, :], tv2[:, :], ALU.subtract)
                    return v.tensor_tensor(Apim[:, :], tv3[:, :], tv4[:, :], ALU.add)
                aft = [f"st1_{fg}"] + ([f"st3_{fg - 1}"] if fg > 0 else [])
                seq.stage("vector", s_tw, nochain=True, after=tuple(aft),
                          signal=f"tw_{fg}")

                def s_step3(t):
                    for (lo, hi, tp) in ((0, 64, (0, 0)), (64, 128, (64, 64))):
                        t.matmul(Xre_ps[lo:hi, :], Wrc[lo:hi, :], Apre[lo:hi, :],
                                 start=True, stop=False, tile_position=tp)
                        t.matmul(Xre_ps[lo:hi, :], WiN[lo:hi, :], Apim[lo:hi, :],
                                 start=False, stop=True, tile_position=tp)
                        t.matmul(Xim_ps[lo:hi, :], Wic[lo:hi, :], Apre[lo:hi, :],
                                 start=True, stop=False, tile_position=tp)
                        last = t.matmul(Xim_ps[lo:hi, :], Wrc[lo:hi, :], Apim[lo:hi, :],
                                        start=False, stop=True, tile_position=tp)
                    return last
                aft = [f"tw_{fg}"]
                if fg > 0:
                    aft.append(f"kx_{fg - 1}" if fg <= 16 else f"pw_{fg - 1}")
                seq.stage("tensor", s_step3, nochain=True, after=tuple(aft),
                          signal=f"st3_{fg}")

            for j in range(8):
                def s_gath(g, sem, j=j):
                    g.indirect_dma_start(
                        out=gxa[:, :], out_offset=None,
                        in_=a1out[:, :],
                        in_offset=bass.IndirectOffsetOnAxis(idxs[:, j:j + 1], 0),
                    ).then_inc(sem, 16)
                seq.stage("gpsimd", s_gath, dma=True,
                          after=("a2a1_done",) if j == 0 else ())

                def s_gsto(sy, sem, j=j):
                    sy.dma_start(out=myvx[j, :, :], in_=gxa[:, :]).then_inc(sem, 16)
                seq.stage("sync", s_gsto, dma=True)

            with ExitStack() as p3k:
                xk = p3k.enter_context(nc.sbuf_tensor("xk", [128, 4096], BF16))

                def s_xkload(sy, sem):
                    src0 = kscr[0:64, :].rearrange("c (p n1) -> p c n1", p=64)
                    src1 = kscr[64:128, :].rearrange("c (p n1) -> p c n1", p=64)
                    for ci in range(4):
                        sy.dma_start(out=xk[0:64, ci * 1024:(ci + 1) * 1024],
                                     in_=src0[:, ci * 16:(ci + 1) * 16, :]
                                     ).then_inc(sem, 16)
                        sy.dma_start(out=xk[64:128, ci * 1024:(ci + 1) * 1024],
                                     in_=src1[:, ci * 16:(ci + 1) * 16, :]
                                     ).then_inc(sem, 16)
                seq.stage("sync", s_xkload, dma=True, ndma=8, signal="xk_ok")

                for g_ in range(16):
                    fwd_group(xk, g_, g_, "xk_ok" if g_ == 0 else None, True)
                    emit_cv()

                    def s_kx(s, g_=g_):
                        s.copy(Draw[:, g_ * 1024: g_ * 1024 + 512], Xre_ps[:, :])
                        return s.copy(Draw[:, g_ * 1024 + 512:(g_ + 1) * 1024],
                                      Xim_ps[:, :])
                    seq.stage("scalar", s_kx, nochain=True,
                              after=(f"st3_{g_}",), signal=f"kx_{g_}")

            xin = sb3("xin", [128, 8192], BF16)
            v2sb = sb3("v2sb", [128, 8192], BF16)

            kr = Draw[:, 0:16384].rearrange("p (g two ck) -> p g two ck",
                                            g=16, two=2, ck=512)

            for sbi in range(2):
                def s_xinload(sy, sem, sbi=sbi):
                    for h in range(2):
                        for q in range(2):
                            shard = 4 * sbi + 2 * h + q
                            src = myvx[shard, :, :].rearrange(
                                "c (p n1) -> p c n1", p=32)
                            for ci in range(4):
                                sy.dma_start(
                                    out=xin[h * 64 + q * 32: h * 64 + q * 32 + 32,
                                            ci * 2048:(ci + 1) * 2048],
                                    in_=src[:, ci * 32:(ci + 1) * 32, :]
                                    ).then_inc(sem, 16)
                seq.stage("sync", s_xinload, dma=True, ndma=16,
                          after=("st1_15",) if sbi == 0 else (),
                          signal=f"xin_ok{sbi}")

                if sbi == 0:
                    kdup = 0
                    for (plo, phi, coff) in ((0, 64, 0), (64, 128, 8192)):
                        for (dlo, dhi) in ((0, 64), (64, 128)):
                            def s_dup(sy, sem, plo=plo, phi=phi, coff=coff,
                                      dlo=dlo, dhi=dhi):
                                sy.dma_start(out=Kre[dlo:dhi, coff:coff + 8192],
                                             in_=kr[plo:phi, :, 0, :]).then_inc(sem, 16)
                                sy.dma_start(out=Kim[dlo:dhi, coff:coff + 8192],
                                             in_=kr[plo:phi, :, 1, :]).then_inc(sem, 16)
                            seq.stage("sync", s_dup, dma=True, ndma=2,
                                      after=("kx_15",) if kdup == 0 else (),
                                      signal="Kready" if kdup == 3 else None)
                            kdup += 1

                for g_ in range(32):
                    fg = 16 + sbi * 32 + g_
                    fwd_group(xin, g_, fg, f"xin_ok{sbi}" if g_ == 0 else None, False)
                    emit_cv()

                    def s_pw(v, g_=g_):
                        ko = 4 * g_ * 128
                        v.tensor_tensor(tv1[:, :], Xre_ps[:, :], Kre[:, ko:ko + 512],
                                        ALU.mult)
                        v.tensor_tensor(tv2[:, :], Xim_ps[:, :], Kim[:, ko:ko + 512],
                                        ALU.mult)
                        v.tensor_tensor(tv3[:, :], Xre_ps[:, :], Kim[:, ko:ko + 512],
                                        ALU.mult)
                        i4 = v.tensor_tensor(tv4[:, :], Xim_ps[:, :], Kre[:, ko:ko + 512],
                                             ALU.mult)
                        seq.tick(v, i4)
                        v.tensor_tensor(Yre[:, :], tv1[:, :], tv2[:, :], ALU.subtract)
                        return v.tensor_tensor(Yim[:, :], tv3[:, :], tv4[:, :], ALU.add)
                    aft = [f"st3_{fg}"]
                    if fg == 16:
                        aft.append("Kready")
                    if fg > 16:
                        aft.append(f"stA_{fg - 1}")
                    seq.stage("vector", s_pw, nochain=True, after=tuple(aft),
                              signal=f"pw_{fg}")

                    def s_stepA(t):
                        for cl in range(4):
                            for h, (lo, hi) in enumerate(((0, 64), (64, 128))):
                                sl = cl if h == 0 else 4 + cl
                                t.matmul(D_ps[:, sl * 128:(sl + 1) * 128],
                                         Yre[lo:hi, cl * 128:(cl + 1) * 128],
                                         mvP[lo:hi, :], start=True, stop=False,
                                         tile_position=(lo, 0))
                                last = t.matmul(D_ps[:, sl * 128:(sl + 1) * 128],
                                                Yim[lo:hi, cl * 128:(cl + 1) * 128],
                                                mvM[lo:hi, :], start=False, stop=True,
                                                tile_position=(lo, 0))
                        return last
                    aft = [f"pw_{fg}"] + ([f"evD_{fg - 1}"] if fg > 16 else [])
                    seq.stage("tensor", s_stepA, nochain=True, after=tuple(aft),
                              signal=f"stA_{fg}")

                    def s_evacD(s, g_=g_):
                        return s.copy(Draw[:, g_ * 1024:(g_ + 1) * 1024], D_ps[:, :])
                    aft = [f"stA_{fg}"]
                    if fg == 16:
                        aft.append("Kready")
                    if sbi == 1 and g_ == 0:
                        aft.append("stC0_15")
                    seq.stage("scalar", s_evacD, nochain=True, after=tuple(aft),
                              signal=f"evD_{fg}")

                Draw4 = Draw[:, :].rearrange("p (g sl two n1) -> p g sl two n1",
                                             g=32, sl=8, two=2, n1=64)
                xin3 = xin[:, :].rearrange("p (c n1) -> p c n1", c=128)
                v23 = v2sb[:, :].rearrange("p (c n1) -> p c n1", c=128)
                for n1b in range(16):
                    def s_stepC(t, n1b=n1b):
                        for n1l in range(4):
                            i1 = n1b * 4 + n1l
                            ure = Ucn[:, i1 * 128: i1 * 128 + 64]
                            uim = Ucn[:, i1 * 128 + 64: i1 * 128 + 128]
                            ysl = lambda lo, hi: y_ps[lo:hi, n1l * 128:(n1l + 1) * 128]
                            t.matmul(ysl(0, 64), ure, Draw4[:, :, 0:4, 0, i1],
                                     start=True, stop=False, tile_position=(0, 0))
                            t.matmul(ysl(0, 64), uim, Draw4[:, :, 0:4, 1, i1],
                                     start=False, stop=True, tile_position=(0, 0))
                            t.matmul(ysl(64, 128), ure, Draw4[:, :, 4:8, 0, i1],
                                     start=True, stop=False, tile_position=(0, 64))
                            last = t.matmul(ysl(64, 128), uim, Draw4[:, :, 4:8, 1, i1],
                                            start=False, stop=True,
                                            tile_position=(0, 64))
                        return last
                    aft = ([f"evD_{16 + sbi * 32 + 31}"] if n1b == 0
                           else [f"gate{sbi}_{n1b - 1}"])
                    if sbi == 1 and n1b == 0:
                        aft.append("gate0_15")
                    seq.stage("tensor", s_stepC, nochain=True, after=tuple(aft),
                              signal=f"stC{sbi}_{n1b}")

                    def s_gate(v, n1b=n1b):
                        t1v = tv1[:, :].rearrange("p (n1l c) -> p n1l c", n1l=4)
                        xslT = xin3[:, :, n1b * 4:n1b * 4 + 4].rearrange("p c n -> p n c")
                        fbv = fbr[:, :].rearrange("p (n1l c) -> p n1l c", n1l=4)
                        i1 = v.tensor_tensor(t1v, xslT, fbv, ALU.mult)
                        seq.tick(v, i1)
                        dstT = v23[:, :, n1b * 4:n1b * 4 + 4].rearrange("p c n -> p n c")
                        ypv = y_ps[:, :].rearrange("p (n1l c) -> p n1l c", n1l=4)
                        return v.tensor_tensor(dstT, ypv, t1v, ALU.add)
                    aft = [f"stC{sbi}_{n1b}"]
                    if sbi == 1 and n1b == 0:
                        aft.append("a2in_ok0")
                    seq.stage("vector", s_gate, nochain=True,
                              after=tuple(aft), signal=f"gate{sbi}_{n1b}")

                def s_v2store(sy, sem, sbi=sbi):
                    for h in range(2):
                        for q in range(2):
                            shard = 4 * sbi + 2 * h + q
                            dst = a2in[shard, :, :].rearrange("c (p n1) -> p c n1", p=32)
                            for ci in range(4):
                                sy.dma_start(
                                    out=dst[:, ci * 32:(ci + 1) * 32, :],
                                    in_=v2sb[h * 64 + q * 32: h * 64 + q * 32 + 32,
                                             ci * 2048:(ci + 1) * 2048]
                                    ).then_inc(sem, 16)
                seq.stage("sync", s_v2store, dma=True, ndma=16,
                          after=(f"gate{sbi}_15",), signal=f"a2in_ok{sbi}")

                def s_a2a2(g, sbi=sbi):
                    return g.collective_compute(
                        "AllGather", ALU.bypass, replica_groups=[list(range(NCORE))],
                        ins=[a2in[4 * sbi:4 * sbi + 4, :, :].opt()],
                        outs=[a2out[4096 * sbi:4096 * (sbi + 1), :].opt()])
                seq.stage("gpsimd", s_a2a2, signal=f"a2a2_done{sbi}", coll=True,
                          after=(f"a2in_ok{sbi}",))

        with ExitStack() as p45:
            ln3T = p45.enter_context(nc.sbuf_tensor("ln3T", [128, 16384], BF16))

            with ExitStack() as p4:
                sb4 = lambda n, s, d: p4.enter_context(nc.sbuf_tensor(n, s, d))
                ps4 = lambda n, s: p4.enter_context(nc.psum_tensor(n, s, F32))
                yvsb = sb4("yvsb", [128, 16384], BF16)
                x0c4 = sb4("x0c4", [128, 2048], BF16)
                v2c = sb4("v2c", [128, 2048], BF16)
                owsb = sb4("owsb", [128, 8192], BF16)
                attnT = sb4("attnT", [128, 16384], BF16)
                attc = [sb4(f"attc{i}", [128, 2048], BF16) for i in range(2)]
                utile4 = [sb4(f"utile4{i}", [128, 1024], F32) for i in range(2)]
                ln3t = [sb4(f"ln3t{i}", [128, 1024], BF16) for i in range(2)]
                x1half = sb4("x1half", [128, 8192], F32)
                stc16 = sb4("stc16", [128, 128], F32)
                o_ps = [ps4(f"o_ps{i}", [128, 2048]) for i in range(2)]

                def s_owload(g, sem):
                    for dk in range(8):
                        g.dma_start(out=owsb[:, dk * 1024:(dk + 1) * 1024],
                                    in_=ap["outwT"][dk * 128:(dk + 1) * 128, :]
                                    ).then_inc(sem, 16)
                seq.stage("gpsimd", s_owload, dma=True, ndma=8, nochain=True,
                          signal="ow_ok")

                for ck in range(8):
                    def s_v2g(g, sem, ck=ck):
                        g.indirect_dma_start(
                            out=v2c[:, :], out_offset=None,
                            in_=a2out[:, :],
                            in_offset=bass.IndirectOffsetOnAxis(idxs2[:, ck:ck + 1], 0),
                        ).then_inc(sem, 16)
                    seq.stage("gpsimd", s_v2g, dma=True,
                              after=("a2a2_done0", "a2a2_done1") if ck == 0 else ())

                    def s_v2l(sy, sem, ck=ck):
                        sy.dma_start(out=x0c4[:, :],
                                     in_=x0sp[:, ck * 2048:(ck + 1) * 2048]
                                     ).then_inc(sem, 16)
                    seq.stage("sync", s_v2l, dma=True)

                    def s_yv(v, ck=ck):
                        return v.tensor_tensor(yvsb[:, ck * 2048:(ck + 1) * 2048],
                                               v2c[:, :], x0c4[:, :], ALU.mult)
                    seq.stage("vector", s_yv, signal="yv7" if ck == 7 else None)

                for ej in range(8):
                    p = ej % 2

                    def s_omm(t, ej=ej, p=p):
                        for ck in range(8):
                            stt = owsb[:, ck * 1024 + ej * 128:
                                       ck * 1024 + (ej + 1) * 128]
                            for ni in range(4):
                                last = t.matmul(o_ps[p][:, ni * 512:(ni + 1) * 512],
                                                stt,
                                                yvsb[:, ck * 2048 + ni * 512:
                                                     ck * 2048 + (ni + 1) * 512],
                                                start=(ck == 0), stop=(ck == 7))
                        return last
                    aft = ["ow_ok", "yv7"] if ej == 0 else [f"oev{ej - 2}"] if ej >= 2 else ["ow_ok", "yv7"]
                    seq.stage("tensor", s_omm, nochain=True, after=tuple(aft),
                              signal=f"omm{ej}")

                    def s_oevac(s, ej=ej, p=p):
                        return s.activation(attc[p][:, :], o_ps[p][:, :], AF.Identity,
                                            bias=colbuf[:, 120 + ej:121 + ej])
                    aft = [f"omm{ej}"] + ([f"otd{ej - 2}"] if ej >= 2 else [])
                    seq.stage("scalar", s_oevac, nochain=True, after=tuple(aft),
                              signal=f"oev{ej}")

                    def s_otdma(sy, sem, ej=ej, p=p):
                        for tt in range(16):
                            sy.dma_start(out=attnT[:, tt * 1024 + ej * 128:
                                                   tt * 1024 + (ej + 1) * 128],
                                         in_=attc[p][:, tt * 128:(tt + 1) * 128],
                                         transpose=True).then_inc(sem, 16)
                    seq.stage("sync", s_otdma, dma=True, ndma=16, nochain=True,
                              after=(f"oev{ej}",), signal=f"otd{ej}")

                at3 = attnT[:, :].rearrange("p (tt d) -> p tt d", tt=16)
                sq3 = yvsb[:, :].rearrange("p (tt d) -> p tt d", tt=16)

                def s_sqA(s):
                    return s.activation(yvsb[:, :], attnT[:, :], AF.Square)
                seq.stage("scalar", s_sqA, after=("otd7",))

                def s_redA(v):
                    v.tensor_reduce(stc16[:, 0:16], at3, AXX, ALU.add)
                    return v.tensor_reduce(stc16[:, 16:32], sq3, AXX, ALU.add)
                seq.stage("vector", s_redA)

                def s_muA4(v):
                    v.tensor_scalar(stc16[:, 32:48], stc16[:, 0:16], fcn[:, 3:4],
                                    None, ALU.mult)
                    return v.tensor_scalar(stc16[:, 48:64], stc16[:, 16:32],
                                           fcn[:, 3:4], None, ALU.mult)
                seq.stage("vector", s_muA4)

                def s_msq4(s):
                    return s.activation(stc16[:, 64:80], stc16[:, 32:48], AF.Square)
                seq.stage("scalar", s_msq4)

                def s_var4(v):
                    return v.tensor_tensor(stc16[:, 80:96], stc16[:, 48:64],
                                           stc16[:, 64:80], ALU.subtract)
                seq.stage("vector", s_var4)

                def s_inv4(s):
                    seq.scl_tick += 1
                    s.activation(stc16[:, 112:128], stc16[:, 80:96], AF.Ln,
                                 bias=fcn[:, 0:1]).then_inc(seq.sems["scl"], 1)
                    s.wait_ge(seq.sems["scl"], seq.scl_tick)
                    return s.activation(stc16[:, 96:112], stc16[:, 112:128], AF.Exp,
                                        scale=fcn[:, 1:2])
                seq.stage("scalar", s_inv4)

                for half in range(2):
                    for tl in range(8):
                        tt = half * 8 + tl

                        def s_url(sy, sem, tt=tt, tl=tl):
                            sy.dma_start(out=utile4[tl % 2][:, :],
                                         in_=ap["u_s"][2 + tt * 128: 2 + (tt + 1) * 128, :]
                                         ).then_inc(sem, 16)
                        seq.stage("sync", s_url, dma=True)

                        def s_res(v, tt=tt, tl=tl):
                            att_t = attnT[:, tt * 1024:(tt + 1) * 1024]
                            i1 = v.tensor_scalar(scr4[:, :], att_t,
                                                 stc16[:, 32 + tt:33 + tt],
                                                 stc16[:, 96 + tt:97 + tt],
                                                 ALU.subtract, ALU.mult)
                            seq.tick(v, i1)
                            return v.tensor_tensor(x1half[:, tl * 1024:(tl + 1) * 1024],
                                                   scr4[:, :], utile4[tl % 2][:, :],
                                                   ALU.add)
                        seq.stage("vector", s_res)

                        def s_x1spl(sy, sem, tt=tt, tl=tl):
                            sy.dma_start(out=x1sp[:, tt * 1024:(tt + 1) * 1024],
                                         in_=x1half[:, tl * 1024:(tl + 1) * 1024]
                                         ).then_inc(sem, 16)
                        seq.stage("sync", s_x1spl, dma=True)

                    x13 = x1half[:, :].rearrange("p (tl d) -> p tl d", tl=8)
                    sqh3 = yvsb[:, 0:8192].rearrange("p (tl d) -> p tl d", tl=8)

                    def s_sqB(s):
                        return s.activation(yvsb[:, 0:8192], x1half[:, :], AF.Square)
                    seq.stage("scalar", s_sqB)

                    def s_redB(v):
                        v.tensor_reduce(stc16[:, 0:8], x13, AXX, ALU.add)
                        return v.tensor_reduce(stc16[:, 8:16], sqh3, AXX, ALU.add)
                    seq.stage("vector", s_redB)

                    def s_muB4(v):
                        v.tensor_scalar(stc16[:, 16:24], stc16[:, 0:8], fcn[:, 3:4],
                                        None, ALU.mult)
                        return v.tensor_scalar(stc16[:, 24:32], stc16[:, 8:16],
                                               fcn[:, 3:4], None, ALU.mult)
                    seq.stage("vector", s_muB4)

                    def s_msqB(s):
                        return s.activation(stc16[:, 48:56], stc16[:, 16:24], AF.Square)
                    seq.stage("scalar", s_msqB)

                    def s_varB(v):
                        return v.tensor_tensor(stc16[:, 56:64], stc16[:, 24:32],
                                               stc16[:, 48:56], ALU.subtract)
                    seq.stage("vector", s_varB)

                    def s_invB(s):
                        seq.scl_tick += 1
                        s.activation(stc16[:, 72:80], stc16[:, 56:64], AF.Ln,
                                     bias=fcn[:, 0:1]).then_inc(seq.sems["scl"], 1)
                        s.wait_ge(seq.sems["scl"], seq.scl_tick)
                        return s.activation(stc16[:, 64:72], stc16[:, 72:80], AF.Exp,
                                            scale=fcn[:, 1:2])
                    seq.stage("scalar", s_invB)

                    for tl in range(8):
                        tt = half * 8 + tl

                        def s_ap3(v, tt=tt, tl=tl):
                            return v.tensor_scalar(ln3t[tl % 2][:, :],
                                                   x1half[:, tl * 1024:(tl + 1) * 1024],
                                                   stc16[:, 16 + tl:17 + tl],
                                                   stc16[:, 64 + tl:65 + tl],
                                                   ALU.subtract, ALU.mult)
                        seq.stage("vector", s_ap3)

                        def s_t3dma(sy, sem, tt=tt, tl=tl):
                            for dk in range(8):
                                sy.dma_start(out=ln3T[:, dk * 2048 + tt * 128:
                                                      dk * 2048 + (tt + 1) * 128],
                                             in_=ln3t[tl % 2][:, dk * 128:(dk + 1) * 128],
                                             transpose=True).then_inc(sem, 16)
                        seq.stage("sync", s_t3dma, dma=True, ndma=8,
                                  signal="p4_done" if tt == 15 else None)

            with ExitStack() as p5:
                sb5 = lambda n, s, d: p5.enter_context(nc.sbuf_tensor(n, s, d))
                ps5 = lambda n, s: p5.enter_context(nc.psum_tensor(n, s, F32))
                gbuf = sb5("gbuf", [128, 16384], BF16)
                hbuf = sb5("hbuf", [128, 16384], BF16)
                wA = [sb5(f"wA{i}", [128, 4096], BF16) for i in range(2)]
                wB = [sb5(f"wB{i}", [128, 1024], BF16) for i in range(2)]
                g1sb = [sb5(f"g1sb{i}", [128, 512], BF16) for i in range(2)]
                t2f = [sb5(f"t2f{i}", [128, 512], F32) for i in range(2)]
                w3c = [sb5(f"w3c{i}", [128, 512], BF16) for i in range(2)]
                w3T = sb5("w3T", [128, 4096], BF16)
                x1tl = sb5("x1tl", [128, 1024], F32)
                otile = sb5("otile", [128, 1024], F32)
                eqb = sb5("eqb", [128, 4096], BF16)
                stc8 = sb5("stc8", [128, 32], F32)
                pA = [ps5(f"pA{i}", [128, 512]) for i in range(2)]
                pB = [ps5(f"pB{i}", [128, 512]) for i in range(2)]

                w3T3 = w3T[:, :].rearrange("p (t d) -> p t d", t=4)
                eqb3 = eqb[:, :].rearrange("p (t d) -> p t d", t=4)

                def emit_output_part(ti):
                    def s_sqE(s):
                        return s.activation(eqb[:, :], w3T[:, :], AF.Square)
                    seq.stage("scalar", s_sqE, after=(f"t3{ti}_7",))

                    def s_redE(v):
                        v.tensor_reduce(stc8[:, 0:4], w3T3, AXX, ALU.add)
                        return v.tensor_reduce(stc8[:, 4:8], eqb3, AXX, ALU.add)
                    seq.stage("vector", s_redE)

                    def s_muE(v):
                        v.tensor_scalar(stc8[:, 8:12], stc8[:, 0:4], fcn[:, 3:4],
                                        None, ALU.mult)
                        return v.tensor_scalar(stc8[:, 12:16], stc8[:, 4:8],
                                               fcn[:, 3:4], None, ALU.mult)
                    seq.stage("vector", s_muE)

                    def s_msqE(s):
                        return s.activation(stc8[:, 16:20], stc8[:, 8:12], AF.Square)
                    seq.stage("scalar", s_msqE)

                    def s_varE(v):
                        return v.tensor_tensor(stc8[:, 20:24], stc8[:, 12:16],
                                               stc8[:, 16:20], ALU.subtract)
                    seq.stage("vector", s_varE)

                    def s_invE(s):
                        seq.scl_tick += 1
                        s.activation(stc8[:, 28:32], stc8[:, 20:24], AF.Ln,
                                     bias=fcn[:, 0:1]).then_inc(seq.sems["scl"], 1)
                        s.wait_ge(seq.sems["scl"], seq.scl_tick)
                        return s.activation(stc8[:, 24:28], stc8[:, 28:32], AF.Exp,
                                            scale=fcn[:, 1:2])
                    seq.stage("scalar", s_invE)

                    for ts_ in range(4):
                        tt = ti * 4 + ts_

                        def s_x1l(sy, sem, tt=tt):
                            sy.dma_start(out=x1tl[:, :],
                                         in_=x1sp[:, tt * 1024:(tt + 1) * 1024]
                                         ).then_inc(sem, 16)
                        seq.stage("sync", s_x1l, dma=True)

                        def s_fin(v, ts_=ts_):
                            wt = w3T[:, ts_ * 1024:(ts_ + 1) * 1024]
                            i1 = v.tensor_scalar(scr4[:, :], wt,
                                                 stc8[:, 8 + ts_:9 + ts_],
                                                 stc8[:, 24 + ts_:25 + ts_],
                                                 ALU.subtract, ALU.mult)
                            seq.tick(v, i1)
                            return v.tensor_tensor(otile[:, :], scr4[:, :], x1tl[:, :],
                                                   ALU.add)
                        seq.stage("vector", s_fin,
                                  signal=f"ofin{ti}" if ts_ == 3 else None)

                        def s_ost(sy, sem, tt=tt):
                            sy.dma_start(out=outp[tt * 128:(tt + 1) * 128, :],
                                         in_=otile[:, :]).then_inc(sem, 16)
                        seq.stage("sync", s_ost, dma=True)

                for ti in range(4):
                    tcol = ti * 512
                    for oj in range(32):
                        p = oj % 2

                        def s_wload(g, sem, oj=oj, p=p):
                            g.dma_start(out=wA[p][:, 0:1024],
                                        in_=w1b[oj, :, :]).then_inc(sem, 16)
                            g.dma_start(out=wB[p][:, :],
                                        in_=w2b[oj, :, :]).then_inc(sem, 16)
                        if oj >= 2:
                            aft = (f"wm{ti}_{oj - 2}",)
                        elif ti > 0:
                            aft = (f"w3m{ti - 1}_{6 + p}",)
                        else:
                            aft = ("p4_done", "cv_all")
                        seq.stage("gpsimd", s_wload, dma=True, ndma=2, nochain=True,
                                  signal=f"wl{ti}_{oj}", after=aft)

                        def s_wmm(t, oj=oj, p=p, tcol=tcol):
                            for dk in range(8):
                                st1 = wA[p][:, dk * 128:(dk + 1) * 128]
                                st2 = wB[p][:, dk * 128:(dk + 1) * 128]
                                mov = ln3T[:, dk * 2048 + tcol: dk * 2048 + tcol + 512]
                                t.matmul(pA[p][:, :], st1, mov,
                                         start=(dk == 0), stop=(dk == 7))
                                last = t.matmul(pB[p][:, :], st2, mov,
                                                start=(dk == 0), stop=(dk == 7))
                            return last
                        aft = [f"wl{ti}_{oj}"]
                        if oj >= 2:
                            aft.append(f"sg{ti}_{oj - 2}")
                        elif ti > 0:
                            aft += [f"e3{ti - 1}_{6 + p}", f"sg{ti - 1}_{30 + p}"]
                        else:
                            aft.append("p4_done")
                        seq.stage("tensor", s_wmm, nochain=True, after=tuple(aft),
                                  signal=f"wm{ti}_{oj}")

                        def s_silu(s, oj=oj, p=p):
                            return s.activation(g1sb[p][:, :], pA[p][:, :], AF.Silu,
                                                scale=colbuf[:, 128 + oj:129 + oj])
                        seq.stage("scalar", s_silu, nochain=True,
                                  after=(f"wm{ti}_{oj}",), signal=f"ss{ti}_{oj}")

                        def s_g(v, oj=oj, p=p):
                            i1 = v.tensor_scalar(t2f[p][:, :], pB[p][:, :],
                                                 colbuf[:, 160 + oj:161 + oj],
                                                 None, ALU.mult)
                            seq.tick(v, i1)
                            return v.tensor_tensor(gbuf[:, oj * 512:(oj + 1) * 512],
                                                   g1sb[p][:, :], t2f[p][:, :],
                                                   ALU.mult)
                        aft = [f"ss{ti}_{oj}"]
                        if oj >= 2:
                            aft.append(f"sg{ti}_{oj - 2}")
                        elif ti > 0:
                            aft.append(f"w3m{ti - 1}_7")
                        seq.stage("vector", s_g, nochain=True, after=tuple(aft),
                                  signal=f"sg{ti}_{oj}")

                    if ti > 0:
                        emit_output_part(ti - 1)

                    for (wb, scol, src, dst, ev, pv) in (
                            (h1b, 192, gbuf, hbuf, "h1", None),
                            (h2b, 224, hbuf, gbuf, "h2", "h1")):
                        for oj in range(32):
                            p = oj % 2

                            def s_hload(g, sem, oj=oj, p=p, wb=wb):
                                g.dma_start(out=wA[p][:, :],
                                            in_=wb[oj, :, :]).then_inc(sem, 16)
                            if oj >= 2:
                                aft = (f"{ev}m{ti}_{oj - 2}",)
                            elif ev == "h1":
                                aft = (f"wm{ti}_{30 + p}",)
                            else:
                                aft = (f"h1m{ti}_{30 + p}",)
                            seq.stage("gpsimd", s_hload, dma=True, ndma=1,
                                      nochain=True, signal=f"{ev}l{ti}_{oj}", after=aft)

                            def s_hmm(t, oj=oj, p=p, src=src):
                                for dk in range(32):
                                    stt = wA[p][:, dk * 128:(dk + 1) * 128]
                                    last = t.matmul(pA[p][:, :], stt,
                                                    src[:, dk * 512:(dk + 1) * 512],
                                                    start=(dk == 0), stop=(dk == 31))
                                return last
                            aft = [f"{ev}l{ti}_{oj}"]
                            if oj >= 2:
                                aft.append(f"ge{ev}{ti}_{oj - 2}")
                            elif ev == "h1":
                                aft += [f"ss{ti}_{30 + p}", f"sg{ti}_31"]
                            else:
                                aft.append(f"geh1{ti}_31")
                            seq.stage("tensor", s_hmm, nochain=True, after=tuple(aft),
                                      signal=f"{ev}m{ti}_{oj}")

                            def s_gelu(s, oj=oj, p=p, scol=scol, dst=dst):
                                return s.activation(dst[:, oj * 512:(oj + 1) * 512],
                                                    pA[p][:, :], AF.Gelu,
                                                    scale=colbuf[:, scol + oj:
                                                                 scol + oj + 1])
                            seq.stage("scalar", s_gelu, nochain=True,
                                      after=(f"{ev}m{ti}_{oj}",),
                                      signal=f"ge{ev}{ti}_{oj}")

                    for ej in range(8):
                        p = ej % 2

                        def s_3load(g, sem, ej=ej, p=p):
                            g.dma_start(out=wA[p][:, :],
                                        in_=w3b[ej, :, :]).then_inc(sem, 16)
                        aft = ((f"w3m{ti}_{ej - 2}",) if ej >= 2
                               else (f"h2m{ti}_{30 + p}",))
                        seq.stage("gpsimd", s_3load, dma=True, ndma=1, nochain=True,
                                  signal=f"w3l{ti}_{ej}", after=aft)

                        def s_3mm(t, ej=ej, p=p):
                            for dk in range(32):
                                stt = wA[p][:, dk * 128:(dk + 1) * 128]
                                last = t.matmul(pA[p][:, :], stt,
                                                gbuf[:, dk * 512:(dk + 1) * 512],
                                                start=(dk == 0), stop=(dk == 31))
                            return last
                        aft = [f"w3l{ti}_{ej}"]
                        if ej >= 2:
                            aft.append(f"e3{ti}_{ej - 2}")
                        else:
                            aft += [f"geh2{ti}_{30 + p}", f"geh2{ti}_31"]
                        seq.stage("tensor", s_3mm, nochain=True, after=tuple(aft),
                                  signal=f"w3m{ti}_{ej}")

                        def s_3evac(v, ej=ej, p=p):
                            return v.tensor_scalar(w3c[p][:, :], pA[p][:, :],
                                                   colb2[:, ej:ej + 1], None, ALU.mult)
                        aft = [f"w3m{ti}_{ej}"]
                        if ej >= 2:
                            aft.append(f"t3{ti}_{ej - 2}")
                        seq.stage("vector", s_3evac, nochain=True, after=tuple(aft),
                                  signal=f"e3{ti}_{ej}")

                        def s_3tdma(sy, sem, ej=ej, p=p):
                            for ts_ in range(4):
                                sy.dma_start(out=w3T[:, ts_ * 1024 + ej * 128:
                                                     ts_ * 1024 + (ej + 1) * 128],
                                             in_=w3c[p][:, ts_ * 128:(ts_ + 1) * 128],
                                             transpose=True).then_inc(sem, 16)
                        aft = [f"e3{ti}_{ej}"]
                        if ej == 0 and ti > 0:
                            aft.append(f"ofin{ti - 1}")
                        seq.stage("sync", s_3tdma, dma=True, ndma=4, nochain=True,
                                  after=tuple(aft), signal=f"t3{ti}_{ej}")



                emit_output_part(3)

        def s_done(g):
            return g.engine_nop()
        seq.stage("gpsimd", s_done)

        if True:
            for e in seq.engines + ["coll", "scl", "gpc"]:
                nc.sync.sem_clear(seq.sems[e])
            nc.all_engine_barrier()
            with nc.Block() as block:
                seq.build(block)
    return nc


_CACHE = {}


def kernel(L0_attn_u, L0_attn_in_w, L0_attn_in_b, L0_attn_short_w, L0_attn_short_b,
           L0_attn_mlp_w1, L0_attn_mlp_b1, L0_attn_mlp_w2, L0_attn_mlp_b2,
           L0_attn_mlp_w3, L0_attn_mlp_b3, L0_attn_mlp_wo, L0_attn_freq,
           L0_attn_filt_bias, L0_attn_out_w, L0_attn_out_b,
           L0_ffn_w1_w, L0_ffn_w1_s, L0_ffn_w2_w, L0_ffn_w2_s,
           L0_ffn_h1_w, L0_ffn_h1_s, L0_ffn_h2_w, L0_ffn_h2_s,
           L0_ffn_w3_w, L0_ffn_w3_s):
    f32 = lambda a: np.ascontiguousarray(np.asarray(a), dtype=np.float32)
    u = f32(L0_attn_u)
    uflat = u.reshape(16384, D)

    chunkcol = lambda vec, n: np.ascontiguousarray(
        np.asarray(vec, np.float32).reshape(n, 128).T)

    def pack(wT, dk, oj):
        i_, o_ = wT.shape
        return np.ascontiguousarray(
            wT.reshape(dk, 128, oj, 128).transpose(2, 1, 0, 3).reshape(oj, 128, dk * 128))

    shared = {
        "inwT": pack(f32(L0_attn_in_w), 8, 24),
        "inb_c": chunkcol(L0_attn_in_b, 24),
        "shb_c": chunkcol(L0_attn_short_b, 24),
        "sw_c": np.ascontiguousarray(np.concatenate(
            [chunkcol(np.asarray(L0_attn_short_w, np.float32)[:, k], 24)
             for k in range(3)], axis=1)),
        "outwT": f32(L0_attn_out_w),
        "outb_c": chunkcol(L0_attn_out_b, 8),
        "w1_t": pack(f32(L0_ffn_w1_w).T, 8, 32),
        "w2_t": pack(f32(L0_ffn_w2_w).T, 8, 32),
        "h1_t": pack(f32(L0_ffn_h1_w).T, 32, 32),
        "h2_t": pack(f32(L0_ffn_h2_w).T, 32, 32),
        "w3_t": pack(f32(L0_ffn_w3_w).T, 32, 8),
        "s1_c": chunkcol(L0_ffn_w1_s, 32),
        "s2_c": chunkcol(L0_ffn_w2_s, 32),
        "sh1_c": chunkcol(L0_ffn_h1_s, 32),
        "sh2_c": chunkcol(L0_ffn_h2_s, 32),
        "s3_c": chunkcol(L0_ffn_w3_s, 8),
        "mlpw1": f32(L0_attn_mlp_w1),
        "mlpw2": f32(L0_attn_mlp_w2),
        "mlpw3": f32(L0_attn_mlp_w3),
        "freq_c": np.ascontiguousarray(f32(L0_attn_freq).reshape(64, 1)),
        "b1_c": np.ascontiguousarray(f32(L0_attn_mlp_b1).reshape(64, 1)),
        "b2_c": np.ascontiguousarray(f32(L0_attn_mlp_b2).reshape(64, 1)),
        "b3_c": np.ascontiguousarray(f32(L0_attn_mlp_b3).reshape(64, 1)),
        "zT": _z_const(),
        "fconst": np.ascontiguousarray(np.tile(np.array(
            [EPS, -0.5, -PI, 1.0 / D, 65.0 * PI, 2.0 * PI, 1.0 / (2.0 * PI), -2.0 * PI],
            np.float32)[None, :], (128, 1))),
    }
    shared.update(_conv_constants())

    wo = f32(L0_attn_mlp_wo)
    fbias = f32(L0_attn_filt_bias)

    in_maps = []
    for r in range(NCORE):
        half = r % 2
        u_s = np.zeros((TPAD, D), np.float32)
        u_s[2:2 + TOK] = uflat[r * TOK:(r + 1) * TOK]
        if half == 1:
            u_s[0:2] = uflat[r * TOK - 2: r * TOK]
        mask = (np.zeros((128, 2), np.float32) if half == 0
                else np.ones((128, 2), np.float32))
        fb_s = fbias[128 * r:128 * r + 128]
        m = dict(shared)
        idxg = np.zeros((128, 8), np.int32)
        for j in range(8):
            idxg[:, j] = 1024 * j + 128 * r + np.arange(128)
        idx2g = np.zeros((128, 8), np.int32)
        base = (0 if r < 4 else 4096) + (r % 4) * 128
        for ck in range(8):
            idx2g[:, ck] = base + ck * 512 + np.arange(128)
        m.update({
            "u_s": u_s,
            "idx_g": idxg,
            "idx2_g": idx2g,
            "mask2": mask,
            "wo_s": np.ascontiguousarray(wo[:, 128 * r:128 * r + 128]),
            "decay_s": _decay_slice(r),
            "fbrep": np.ascontiguousarray(
                np.tile(np.tile(fb_s, 4)[None, :], (128, 1)).astype(np.float32)),
        })
        in_maps.append(m)

    if "nc" not in _CACHE:
        _CACHE["nc"] = build_nc()
    res = run_bass_kernel_spmd(_CACHE["nc"], in_maps, core_ids=list(range(NCORE)))
    out = np.concatenate([np.asarray(res.results[r]["outp"], dtype=np.float32)
                          for r in range(NCORE)], axis=0)
    return out.reshape(4, 4096, 1024)
